# version 16
# baseline (speedup 1.0000x reference)
import numpy as np
import ml_dtypes
import concourse.bass as bass
import concourse.mybir as mybir
from concourse.bass_utils import run_bass_kernel_spmd

BFNP = ml_dtypes.bfloat16
F32 = mybir.dt.float32
BF16 = mybir.dt.bfloat16
I32 = mybir.dt.int32
U8 = mybir.dt.uint8
AF = mybir.ActivationFunctionType
ALU = mybir.AluOpType

NCORES = 8
TOK = 1024
H = 2048
DFF = 8192
NHB = H // 128
NIB = DFF // 128
T = 128
NCH = TOK // T
NPH = 8
PHI = DFF // NPH
SPAN = 256
NSP = PHI // SPAN
GRP = 512
NG = TOK // GRP
EPS = 1e-5
TWO_PI = float(2 * np.pi)
INV_2PI = float(1.0 / (2 * np.pi))
HALF_PI = float(np.pi / 2)
RW_W = 512


class Sem:
    def __init__(self, nc, name):
        self.h = nc.alloc_semaphore(name)
        self.n = 0

    def inc(self, instr, k=1):
        instr.then_inc(self.h, k)
        self.n += k
        return self.n


def build_program(taps=()):
    nc = bass.Bass("TRN2", target_bir_lowering=False, debug=False)

    d_xT = nc.dram_tensor("xT", [H, TOK], F32, kind="ExternalInput").ap()
    d_wgT = nc.dram_tensor("wgT", [H, DFF], U8, kind="ExternalInput").ap()
    d_wuT = nc.dram_tensor("wuT", [H, DFF], U8, kind="ExternalInput").ap()
    d_wdT = nc.dram_tensor("wdT", [DFF, H], U8, kind="ExternalInput").ap()
    d_gs = nc.dram_tensor("gsB", [128, NIB], F32, kind="ExternalInput").ap()
    d_us = nc.dram_tensor("usB", [128, NIB], F32, kind="ExternalInput").ap()
    d_ds = nc.dram_tensor("dsB", [128, NHB], F32, kind="ExternalInput").ap()
    d_DB = nc.dram_tensor("DB", [128, NHB], F32, kind="ExternalInput").ap()
    d_dg = nc.dram_tensor("Ddiag", [H, 128], BF16, kind="ExternalInput").ap()
    d_BreT = nc.dram_tensor("BreT", [H, 32], F32, kind="ExternalInput").ap()
    d_BimT = nc.dram_tensor("BimT", [H, 32], F32, kind="ExternalInput").ap()
    d_CreT = nc.dram_tensor("CreT", [32, H], F32, kind="ExternalInput").ap()
    d_CimT = nc.dram_tensor("CimT", [32, H], F32, kind="ExternalInput").ap()
    d_Lu = nc.dram_tensor("Lu", [1, RW_W], F32, kind="ExternalInput").ap()
    d_Lim = nc.dram_tensor("Lim", [1, RW_W], F32, kind="ExternalInput").ap()
    d_logD = nc.dram_tensor("logD", [1, RW_W], F32, kind="ExternalInput").ap()
    d_sel = nc.dram_tensor("sel32", [32, 8], F32, kind="ExternalInput").ap()
    d_iotab = nc.dram_tensor("iotaB", [32, TOK], BF16,
                             kind="ExternalInput").ap()
    d_scmask = nc.dram_tensor("scMask", [32, TOK], F32,
                              kind="ExternalInput").ap()
    d_onesw = nc.dram_tensor("onesW", [64, T], F32, kind="ExternalInput").ap()
    d_ones1 = nc.dram_tensor("onesB", [128, 1], BF16,
                             kind="ExternalInput").ap()
    d_onesR = nc.dram_tensor("onesR", [1, 128], F32, kind="ExternalInput").ap()
    d_yT = nc.dram_tensor("yT", [H, TOK], F32, kind="ExternalOutput").ap()
    d_taps = {}
    for t_ in taps:
        d_taps[t_] = nc.dram_tensor("tap_" + t_, [H, TOK], F32,
                                    kind="ExternalOutput").ap()
    cc_in = nc.dram_tensor("cc_in", [1, 64], F32, kind="Internal").ap()
    cc_out = nc.dram_tensor("cc_out", [NCORES, 64], F32, kind="Internal").ap()
    d_scr = nc.dram_tensor("scrA", [1, 1024], F32, kind="Internal").ap()

    A = nc.alloc_sbuf_tensor
    R = A("R", [128, NHB * TOK], F32)
    XT = A("XT", [128, NHB * TOK], BF16)
    HT = A("HT", [128, (PHI // 128) * TOK], BF16)
    WG = A("WG", [128, 2 * 16 * SPAN], BF16)
    WU = A("WU", [128, 2 * 16 * SPAN], BF16)
    WD = A("WD", [128, (PHI // 128) * 512], BF16)
    STG = A("STG", [128, 16384], U8)
    TS = A("TS", [128, 2048], BF16)
    C2 = A("C2", [64, H], BF16)
    BBARR = A("BBARR", [128, 512], BF16)
    BBARI = A("BBARI", [128, 512], BF16)
    XSB = A("XSB", [64, TOK], BF16)
    DG = A("DG", [128, 256], BF16)
    BRE = A("BRE", [128, NHB * 32], F32)
    BIM = A("BIM", [128, NHB * 32], F32)
    IOTAB = A("IOTAB", [32, TOK], BF16)
    SCMASK = A("SCMASK", [32, TOK], F32)
    ONESW = A("ONESW", [64, T], F32)
    SEL32 = A("SEL32", [32, 8], F32)
    COLS4 = A("COLS4", [32, 4], F32)
    EG32 = A("EG32", [32, 16], F32)
    GS = A("GS", [128, NIB], F32)
    US = A("US", [128, NIB], F32)
    DS2 = A("DS2", [128, NHB], F32)
    DB = A("DBs", [128, NHB], F32)
    LUR = A("LUR", [1, RW_W], F32)
    LIMR = A("LIMR", [1, RW_W], F32)
    LOGDR = A("LOGDR", [1, RW_W], F32)
    ONE1 = A("ONE1", [128, 1], BF16)
    ONER = A("ONER", [1, 128], F32)
    EPS1 = A("EPS1", [1, 1], F32)
    NEGH = A("NEGH", [1, 1], F32)
    NEG1 = A("NEG1", [1, 1], F32)
    ONEF = A("ONEF", [1, 1], F32)
    ZERO32 = A("ZERO32", [32, 1], F32)
    DBG = A("DBG", [128, 512], F32)
    DUMV = A("DUMV", [1, 128], F32)
    DUMS = A("DUMS", [1, 128], F32)
    SCR = A("SCRROW", [1, 512], F32)

    HTF = HT[:].bitcast(F32)
    STF = STG[:].bitcast(F32)
    WDF = WD[:].bitcast(F32)
    TSF = TS[:].bitcast(F32)
    DBGI = DBG[:].bitcast(I32)

    def srow(i):
        return HTF[0:1, 2048 + i * 128:2048 + (i + 1) * 128]

    def wrow(i):
        return WDF[0:1, i * RW_W:(i + 1) * RW_W]

    r_lr, r_li, r_sp, r_dl, r_t1, r_t2, r_ar, r_ai = [srow(i)
                                                      for i in range(8)]
    r_w1, r_w2, r_fr, r_fi = [wrow(i) for i in range(4)]
    KIROW = DBGI[0:1, 0:128]

    def c128(ap):
        return ap[0:1, 0:128]

    def s32(i):
        return STF[0:32, i * TOK:(i + 1) * TOK]

    def h32(i):
        return HTF[0:32, i * TOK:(i + 1) * TOK]

    BuR, BuI, CSR, CSI = [s32(i) for i in range(4)]
    Ef, Cf, Sf, QI = [h32(i) for i in range(4)]
    ZR, ZI = BuR, QI
    XSR, XSI = CSR, CSI
    KIW = STF[0:32, 3 * TOK:4 * TOK].bitcast(I32)

    def ctileT(i):
        return TSF[0:32, i * T:(i + 1) * T]

    def ctileB(i):
        return STF[0:32, TOK + i * T:TOK + (i + 1) * T]

    c_cwR, c_cwI, c_t1, c_t2, c_t3, c_t4, c_u1, c_u2 = \
        [ctileT(i) for i in range(8)]
    c_A127r, c_A127i, c_A128r, c_A128i, c_A10r, c_A10i, c_ARw, c_AIw = \
        [ctileB(i) for i in range(8)]
    c_CwR = DBG[0:32, 0:128]
    c_CwI = DBG[0:32, 128:256]
    c_SR = DBG[0:32, 256:384]
    c_SI = DBG[0:32, 384:512]
    C0R = WDF[0:32, 0:1024]
    C0I = WDF[0:32, 1024:2048]
    c_acR, c_acI = c_SR, c_SI
    c_e1, c_e2 = c_t3, c_t4
    FRBR = TSF[0:128, 0:512]
    FIBR = TSF[0:128, 512:1024]

    P = [nc.alloc_psum_tensor(f"P{i}", [128, 512], F32) for i in range(8)]

    s_w = Sem(nc, "w")
    s_ws = [Sem(nc, f"ws{i}") for i in range(8)]
    s_wds = [Sem(nc, f"wds{i}") for i in range(4)]
    s_rq = [Sem(nc, f"rq{i}") for i in range(4)]
    s_dg = [Sem(nc, f"dg{i}") for i in range(2)]
    s_xc = Sem(nc, "xc")
    s_hdr = Sem(nc, "hdr")
    s_cv = Sem(nc, "cv")
    s_cvr = Sem(nc, "cvr")
    s_htr = Sem(nc, "htr")
    s_cvdr = Sem(nc, "cvdr")
    s_wf = Sem(nc, "wf")
    s_cvd = Sem(nc, "cvd")
    s_gu = Sem(nc, "gu")
    s_t0 = Sem(nc, "t0")
    s_ht = Sem(nc, "ht")
    s_mmd = Sem(nc, "mmd")
    s_hd = Sem(nc, "hd")
    s_yd = Sem(nc, "yd")
    s_sq = Sem(nc, "sq")
    s_mv = Sem(nc, "mv")
    s_var = Sem(nc, "var")
    s_sqd = Sem(nc, "sqd")
    s_rst = Sem(nc, "rst")
    s_bc = Sem(nc, "bc")
    s_xt = Sem(nc, "xt")
    s_mm = Sem(nc, "mm")
    s_ve = Sem(nc, "ve")
    s_ac = Sem(nc, "ac")
    s_g = Sem(nc, "g")
    s_st = Sem(nc, "st")
    s_bar = Sem(nc, "bar")

    blk = nc.Block("main").__enter__()
    ve_init = [0]

    def barrier():
        tgt = s_bar.n + 5

        def fv(e, tgt=tgt):
            e.wait_ge(s_ve.h, ve_init[0])
            e.tensor_copy(out=DUMV[0:1, :],
                          in_=DUMV[0:1, :]).then_inc(s_bar.h, 1)
            e.wait_ge(s_bar.h, tgt)

        def fs(e, tgt=tgt):
            e.wait_ge(s_ve.h, ve_init[0])
            e.activation(out=DUMS[0:1, :], in_=DUMS[0:1, :],
                         func=AF.Identity).then_inc(s_bar.h, 1)
            e.wait_ge(s_bar.h, tgt)

        def f0(e, tgt=tgt):
            e.sem_inc(s_bar.h, 1)
            e.wait_ge(s_bar.h, tgt)

        blk.sync(f0)
        blk.tensor(f0)
        blk.vector(fv)
        blk.scalar(fs)
        blk.gpsimd(f0)
        s_bar.n = tgt

    def s0(sync):
        for q in range(4):
            dst = R[:, q * 4 * TOK:(q + 1) * 4 * TOK].rearrange(
                "p (b t) -> p b t", b=4)
            src = d_xT[q * 512:(q + 1) * 512, :].rearrange(
                "(b p) t -> p b t", p=128)
            s_rq[q].inc(sync.dma_start(out=dst, in_=src), 16)
        loads = [
            (GS[:], d_gs), (US[:], d_us), (DB[:], d_DB), (DS2[:], d_ds),
            (BRE[:].rearrange("p (b n) -> p b n", n=32),
             d_BreT.rearrange("(b p) n -> p b n", p=128)),
            (BIM[:].rearrange("p (b n) -> p b n", n=32),
             d_BimT.rearrange("(b p) n -> p b n", p=128)),
            (HTF[0:32, 0:2048], d_CreT),
            (HTF[0:32, 2048:4096], d_CimT),
            (LUR[:], d_Lu), (LIMR[:], d_Lim), (LOGDR[:], d_logD),
            (SEL32[:], d_sel), (IOTAB[:], d_iotab), (SCMASK[:], d_scmask),
            (ONESW[:], d_onesw), (ONE1[:], d_ones1), (ONER[:], d_onesR),
        ]
        for dst, src in loads:
            s_w.inc(sync.dma_start(out=dst, in_=src), 16)

    blk.sync(s0)
    n_loads_done = s_w.n

    def s0v(vector):
        vector.memset(EPS1[:], EPS)
        vector.memset(NEGH[:], -0.5)
        vector.memset(NEG1[:], -1.0)
        vector.memset(ONEF[:], 1.0)
        vector.memset(ZERO32[:], 0.0)
        vector.memset(DUMV[:], 0.0)
        vector.wait_ge(s_w.h, n_loads_done)
        vector.tensor_scalar(out=DS2[:], in0=DS2[:], scalar1=0.5,
                             scalar2=None, op0=ALU.mult)
        vector.tensor_scalar(out=C2[0:32, :], in0=HTF[0:32, 0:2048],
                             scalar1=2.0, scalar2=None, op0=ALU.mult)
        vector.tensor_scalar(out=C2[32:64, :], in0=HTF[0:32, 2048:4096],
                             scalar1=-2.0, scalar2=None, op0=ALU.mult)
        s_ve.inc(vector.memset(DUMS[:], 0.0))

    blk.vector(s0v)
    ve_init[0] = s_ve.n
    barrier()

    def prep_s1(scalar):
        scalar.activation(out=r_t1, in_=c128(LUR[:]), func=AF.Abs)
        scalar.activation(out=r_t2, in_=r_t1, func=AF.Exp, scale=NEG1[0:1, :])
        scalar.activation(out=r_t2, in_=r_t2, func=AF.Ln,
                          bias=ONEF[0:1, :])
        scalar.activation(out=r_dl, in_=c128(LOGDR[:]), func=AF.Exp)
        scalar.drain()
        scalar.sem_inc(s_ac.h, 2)
        s_ac.n += 2

    blk.scalar(prep_s1)

    def wrap_row(vector, arg, out, ktmp):
        vector.tensor_scalar(out=out, in0=arg, scalar1=INV_2PI, scalar2=None,
                             op0=ALU.mult)
        vector.tensor_copy(out=ktmp, in_=out)
        vector.tensor_copy(out=out, in_=ktmp)
        vector.scalar_tensor_tensor(out=out, in0=out, scalar=-TWO_PI, in1=arg,
                                    op0=ALU.mult, op1=ALU.add)

    def prep_v1(vector):
        vector.wait_ge(s_ac.h, s_ac.n)
        vector.tensor_scalar(out=r_sp, in0=c128(LUR[:]), scalar1=0.0,
                             scalar2=None, op0=ALU.max)
        vector.tensor_tensor(out=r_sp, in0=r_sp, in1=r_t2, op=ALU.add)
        vector.tensor_scalar(out=r_t1, in0=r_sp, scalar1=0.0101, scalar2=-1.0,
                             op0=ALU.add, op1=ALU.mult)
        vector.tensor_tensor(out=r_lr, in0=r_t1, in1=r_dl, op=ALU.mult)
        vector.tensor_tensor(out=r_li, in0=c128(LIMR[:]), in1=r_dl,
                             op=ALU.mult)
        vector.tensor_tensor(out=c128(r_w1), in0=r_t1, in1=r_t1, op=ALU.mult)
        vector.tensor_tensor(out=c128(r_w2), in0=c128(LIMR[:]),
                             in1=c128(LIMR[:]), op=ALU.mult)
        vector.tensor_tensor(out=r_sp, in0=c128(r_w1), in1=c128(r_w2),
                             op=ALU.add)
        vector.reciprocal(out=r_sp, in_=r_sp)
        wrap_row(vector, r_li, c128(r_w1), KIROW)
        vector.tensor_scalar(out=r_t2, in0=r_li, scalar1=HALF_PI,
                             scalar2=None, op0=ALU.add)
        wrap_row(vector, r_t2, c128(r_w2), KIROW)
        vector.drain()
        vector.sem_inc(s_ve.h, 1)
        s_ve.n += 1

    blk.vector(prep_v1)

    def prep_s2(scalar):
        scalar.wait_ge(s_ve.h, s_ve.n)
        scalar.activation(out=r_ar, in_=r_lr, func=AF.Exp)
        scalar.activation(out=c128(r_w1), in_=c128(r_w1),
                          func=AF.Sin)
        scalar.activation(out=c128(r_w2), in_=c128(r_w2),
                          func=AF.Sin)
        scalar.drain()
        scalar.sem_inc(s_ac.h, 1)
        s_ac.n += 1

    blk.scalar(prep_s2)

    def prep_v2(vector):
        vector.wait_ge(s_ac.h, s_ac.n)
        vector.tensor_tensor(out=r_ai, in0=r_ar, in1=c128(r_w1), op=ALU.mult)
        vector.tensor_tensor(out=r_ar, in0=r_ar, in1=c128(r_w2), op=ALU.mult)
        vector.tensor_scalar(out=c128(r_w1), in0=r_ar, scalar1=-1.0,
                             scalar2=None, op0=ALU.add)
        vector.tensor_tensor(out=c128(r_fr), in0=c128(r_w1), in1=r_t1,
                             op=ALU.mult)
        vector.tensor_tensor(out=c128(r_w2), in0=r_ai, in1=c128(LIMR[:]),
                             op=ALU.mult)
        vector.tensor_tensor(out=c128(r_fr), in0=c128(r_fr), in1=c128(r_w2),
                             op=ALU.add)
        vector.tensor_tensor(out=c128(r_fr), in0=c128(r_fr), in1=r_sp,
                             op=ALU.mult)
        vector.tensor_tensor(out=c128(r_fi), in0=r_ai, in1=r_t1, op=ALU.mult)
        vector.tensor_tensor(out=c128(r_w2), in0=c128(r_w1), in1=c128(LIMR[:]),
                             op=ALU.mult)
        vector.tensor_tensor(out=c128(r_fi), in0=c128(r_fi), in1=c128(r_w2),
                             op=ALU.subtract)
        vector.tensor_tensor(out=c128(r_fi), in0=c128(r_fi), in1=r_sp,
                             op=ALU.mult)
        vector.drain()
        vector.sem_inc(s_ve.h, 1)
        s_ve.n += 1

    blk.vector(prep_v2)
    prep_rows_done = s_ve.n

    def prep_g1(gpsimd):
        gpsimd.wait_ge(s_ve.h, prep_rows_done)
        for q, rowap in enumerate((r_lr, r_li, r_ar, r_ai)):
            s_g.inc(gpsimd.dma_start(out=d_scr[0:1, q * 32:(q + 1) * 32],
                                     in_=rowap[0:1, 0:32]), 16)
        gpsimd.wait_ge(s_g.h, s_g.n)
        s_g.inc(gpsimd.dma_start(
            out=COLS4[:],
            in_=d_scr[0:1, 0:128].rearrange("p (q n) -> n (p q)", n=32),
            allow_slow_non_contiguous=True), 16)
        s_g.inc(gpsimd.dma_start(out=d_scr[0:1, 128:160],
                                 in_=r_fr[0:1, 0:32]), 16)
        gpsimd.wait_ge(s_g.h, s_g.n)
        w = 32
        while w < 512:
            s_g.inc(gpsimd.dma_start(out=d_scr[0:1, 128 + w:128 + 2 * w],
                                     in_=d_scr[0:1, 128:128 + w]), 16)
            gpsimd.wait_ge(s_g.h, s_g.n)
            w *= 2
        s_g.inc(gpsimd.dma_start(out=SCR[0:1, 0:512],
                                 in_=d_scr[0:1, 128:640]), 16)
        gpsimd.wait_ge(s_g.h, s_g.n)
        s_g.inc(gpsimd.dma_start(out=d_scr[0:1, 128:160],
                                 in_=r_fi[0:1, 0:32]), 16)
        gpsimd.wait_ge(s_g.h, s_g.n)
        w = 32
        while w < 512:
            s_g.inc(gpsimd.dma_start(out=d_scr[0:1, 128 + w:128 + 2 * w],
                                     in_=d_scr[0:1, 128:128 + w]), 16)
            gpsimd.wait_ge(s_g.h, s_g.n)
            w *= 2
        s_g.inc(gpsimd.dma_start(out=r_w1[0:1, 0:512],
                                 in_=d_scr[0:1, 128:640]), 16)
        gpsimd.wait_ge(s_g.h, s_g.n)

    blk.gpsimd(prep_g1)
    prep_cols_done = s_g.n

    def prep_t1(tensor):
        tensor.wait_ge(s_g.h, prep_cols_done)
        tensor.matmul(out=P[2][:, :], lhsT=ONER[:], rhs=SCR[0:1, 0:512],
                      start=True, stop=True)
        s_mm.inc(tensor.matmul(out=P[3][:, :], lhsT=ONER[:],
                               rhs=r_w1[0:1, 0:512], start=True, stop=True))

    blk.tensor(prep_t1)
    prep_mm_done = s_mm.n

    def prep_v3(vector):
        vector.wait_ge(s_mm.h, prep_mm_done)
        vector.tensor_tensor(out=DBG[:], in0=P[2][:, :], in1=BRE[:],
                             op=ALU.mult)
        vector.tensor_tensor(out=BBARR[:], in0=P[3][:, :], in1=BIM[:],
                             op=ALU.mult)
        vector.tensor_tensor(out=BBARR[:], in0=DBG[:], in1=BBARR[:],
                             op=ALU.subtract)
        vector.tensor_tensor(out=DBG[:], in0=P[2][:, :], in1=BIM[:],
                             op=ALU.mult)
        vector.tensor_tensor(out=BBARI[:], in0=P[3][:, :], in1=BRE[:],
                             op=ALU.mult)
        vector.tensor_tensor(out=BBARI[:], in0=DBG[:], in1=BBARI[:],
                             op=ALU.add)
        vector.drain()
        vector.sem_inc(s_ve.h, 1)
        s_ve.n += 1

    blk.vector(prep_v3)
    prep_done = s_ve.n

    def emit_ln(wait_rq=None, prefetch=None):
        mrow = HTF[0:1, 0:1024]
        vrow = HTF[0:1, 1024:2048]
        xc_base = s_xc.n

        def vcopy(vector):
            for q in range(4):
                if wait_rq is not None:
                    wait_rq(vector, q)
                for b in range(q * 4, q * 4 + 4):
                    vector.tensor_copy(out=XT[:, b * TOK:(b + 1) * TOK],
                                       in_=R[:, b * TOK:(b + 1) * TOK])
                vector.drain()
                vector.sem_inc(s_xc.h, 1)
                s_xc.n += 1

        blk.vector(vcopy)
        if prefetch is not None:
            prefetch()
        sq_base = s_sq.n
        mv_base = s_mv.n

        def scopy(scalar):
            for b in range(NHB):
                if b % 4 == 0 and wait_rq is not None:
                    wait_rq(scalar, b // 4)
                if b >= 2:
                    scalar.wait_ge(s_mv.h, mv_base + (b - 1))
                scalar.activation(
                    out=TS[:, (b % 2) * 1024:(b % 2) * 1024 + 1024],
                    in_=R[:, b * TOK:(b + 1) * TOK], func=AF.Square)
                scalar.drain()
                scalar.sem_inc(s_sq.h, 1)
                s_sq.n += 1

        blk.scalar(scopy)
        psum_guards = [(s_ht, s_ht.n), (s_t0, s_t0.n),
                       (s_hd, s_hd.n), (s_yd, s_yd.n)]

        def tstats(tensor):
            for sem, val in psum_guards:
                if val:
                    tensor.wait_ge(sem.h, val)
            for b in range(NHB):
                if b % 4 == 0:
                    tensor.wait_ge(s_xc.h, xc_base + b // 4 + 1)
                for g in range(NG):
                    tensor.matmul(
                        out=P[4 + g][0:1, :], lhsT=ONE1[:],
                        rhs=XT[:, b * TOK + g * GRP:b * TOK + g * GRP + GRP],
                        start=(b == 0), stop=(b == NHB - 1))
            for b in range(NHB):
                tensor.wait_ge(s_sq.h, sq_base + b + 1)
                last = (b == NHB - 1)
                tensor.matmul(
                    out=P[6][0:1, :], lhsT=ONE1[:],
                    rhs=TS[:, (b % 2) * 1024:(b % 2) * 1024 + GRP],
                    start=(b == 0), stop=last)
                s_mv.inc(tensor.matmul(
                    out=P[7][0:1, :], lhsT=ONE1[:],
                    rhs=TS[:, (b % 2) * 1024 + GRP:(b % 2) * 1024 + 2 * GRP],
                    start=(b == 0), stop=last))

        blk.tensor(tstats)
        stat_target = s_mv.n

        def vrows(vector):
            vector.wait_ge(s_mv.h, stat_target)
            for g in range(NG):
                m = mrow[0:1, g * GRP:(g + 1) * GRP]
                v = vrow[0:1, g * GRP:(g + 1) * GRP]
                vector.tensor_scalar(out=m, in0=P[4 + g][0:1, :],
                                     scalar1=1.0 / H, scalar2=None,
                                     op0=ALU.mult)
                vector.tensor_tensor(out=SCR[0:1, 0:GRP], in0=m, in1=m,
                                     op=ALU.mult)
                vector.scalar_tensor_tensor(out=v, in0=P[6 + g][0:1, :],
                                            scalar=1.0 / H,
                                            in1=SCR[0:1, 0:GRP],
                                            op0=ALU.mult, op1=ALU.subtract)
            vector.drain()
            vector.sem_inc(s_var.h, 1)
            s_var.n += 1

        blk.vector(vrows)
        var_target = s_var.n

        def ssqrt(scalar):
            scalar.wait_ge(s_var.h, var_target)
            scalar.activation(out=vrow[0:1, 0:1024], in_=vrow[0:1, 0:1024],
                              func=AF.Ln, bias=EPS1[0:1, :])
            scalar.activation(out=vrow[0:1, 0:1024], in_=vrow[0:1, 0:1024],
                              func=AF.Exp, scale=NEGH[0:1, :])
            scalar.drain()
            scalar.sem_inc(s_sqd.h, 1)
            s_sqd.n += 1

        blk.scalar(ssqrt)
        sqd_target = s_sqd.n

        def tbcast(tensor):
            tensor.wait_ge(s_sqd.h, sqd_target)
            for g in range(NG):
                tensor.matmul(out=P[4 + g][:, :], lhsT=ONER[:],
                              rhs=mrow[0:1, g * GRP:(g + 1) * GRP],
                              start=True, stop=True)
            for g in range(NG):
                ins = tensor.matmul(out=P[6 + g][:, :], lhsT=ONER[:],
                                    rhs=vrow[0:1, g * GRP:(g + 1) * GRP],
                                    start=True, stop=True)
            s_bc.inc(ins)

        blk.tensor(tbcast)
        bc_target = s_bc.n

        def vnorm(vector):
            vector.wait_ge(s_bc.h, bc_target)
            for b in range(NHB):
                for g in range(NG):
                    sl = slice(b * TOK + g * GRP, b * TOK + g * GRP + GRP)
                    vector.tensor_tensor(out=XT[:, sl], in0=R[:, sl],
                                         in1=P[4 + g][:, :], op=ALU.subtract)
                    vector.tensor_tensor(out=XT[:, sl], in0=XT[:, sl],
                                         in1=P[6 + g][:, :], op=ALU.mult)
                if b % 2 == 1:
                    vector.drain()
                    vector.sem_inc(s_xt.h, 1)
                    s_xt.n += 1

        xt_base = s_xt.n
        blk.vector(vnorm)
        return xt_base

    st = dict(gu_dma=0, span=0, nblk=0, ddma=0, dcvt=0, dhs=0, dunit=0,
              wf_at_span=[], cv_at_span=[], gu_at_block=[], t0_at=[],
              ht_at_block=[], cvd_at_hs=[], mmd_at_phase=[], call=0,
              pending_vmul=None)

    def emit_gu_stage(gspan, extra=None):
        local_s = gspan % (NPH * NSP)
        i0 = (local_s // NSP) * PHI + (local_s % NSP) * SPAN
        buf = gspan % 2

        def sdma(sync):
            if extra is not None:
                sync.wait_ge(extra[0].h, extra[1])
            for dmat in (d_wgT, d_wuT):
                for u in range(4):
                    idx = st["gu_dma"]
                    slot = idx % 8
                    if idx >= 8:
                        sync.wait_ge(s_cv.h, idx - 8 + 1)
                    src = dmat[4 * u * 128:(4 * u + 4) * 128,
                               i0:i0 + SPAN]
                    src = src.rearrange("(c p) i -> p c i", p=128)
                    dst = STG[:, slot * 1024:(slot + 1) * 1024]
                    dst = dst.rearrange("p (c i) -> p c i", c=4)
                    s_ws[slot].inc(sync.dma_start(out=dst, in_=src), 16)
                    st["gu_dma"] += 1

        blk.sync(sdma)

        def gcvt(vector):
            for mi, wdst in enumerate((WG, WU)):
                for u in range(4):
                    idx = gspan * 8 + mi * 4 + u
                    slot = idx % 8
                    vector.wait_ge(s_ws[slot].h, 16 * (idx // 8 + 1))
                    if gspan >= 2 and mi == 0 and u == 0:
                        vector.wait_ge(s_wf.h,
                                       st["wf_at_span"][gspan - 2])
                    src = STG[:, slot * 1024:(slot + 1) * 1024]
                    dst = wdst[:, buf * 4096 + u * 1024:
                               buf * 4096 + (u + 1) * 1024]
                    s_cv.inc(vector.tensor_scalar(
                        out=dst, in0=src, scalar1=-128, scalar2=None,
                        op0=ALU.add))
            vector.drain()
            vector.sem_inc(s_cvr.h, 1)
            s_cvr.n += 1

        blk.vector(gcvt)
        st["cv_at_span"].append(s_cv.n)
        st.setdefault("cvr_at_span", []).append(s_cvr.n)

    def emit_ffn(xt_target, first_dma_extra_wait=None, n_pre=0):
        first_span_of_call = st["span"]
        nspans = NPH * NSP

        def emit_dcvt_hb(vector, G, hb):
            if hb == 0:
                for u in range(2):
                    vector.wait_ge(s_wds[(G % 2) * 2 + u].h,
                                   16 * (G // 2 + 1))
            if G >= 1:
                vector.wait_ge(s_mmd.h, 8 * (G - 1) + hb * 2 + 2)
            for u in range(2):
                base8 = 8192 + (G % 2) * 4096 + u * 2048
                src = STG[:, base8:base8 + 2048]
                src = src.rearrange("p (c h) -> p c h", h=512)
                s_cvd.inc(vector.tensor_scalar(
                    out=WD[:, hb * 1024 + u * 512:
                           hb * 1024 + u * 512 + 512],
                    in0=src[:, :, hb * 128:(hb + 1) * 128],
                    scalar1=-128, scalar2=None, op0=ALU.add))
            vector.drain()
            vector.sem_inc(s_cvdr.h, 1)
            s_cvdr.n += 1

        if n_pre == 0:
            emit_gu_stage(first_span_of_call, extra=first_dma_extra_wait)
            n_pre = 1
        for local_s in range(nspans):
            gspan = st["span"]
            ph = local_s // NSP
            sp_ = local_s % NSP
            phase_global = len(st["mmd_at_phase"])
            buf = gspan % 2
            if n_pre <= local_s + 1 < nspans:
                emit_gu_stage(first_span_of_call + local_s + 1)
            if sp_ == 1 or sp_ == NSP - 1:
                hs_pair = (0, 1) if sp_ == 1 else (2, 3)

                def ddma2(sync, ph=ph, hs_pair=hs_pair, base=st["ddma"]):
                    idxd = base
                    for hs in hs_pair:
                        for u in range(2):
                            Gw = idxd // 2
                            slot = (Gw % 2) * 2 + u
                            if Gw >= 2:
                                sync.wait_ge(s_cvd.h, 8 * (Gw - 1))
                            src = d_wdT[ph * PHI + 4 * u * 128:
                                        ph * PHI + (4 * u + 4) * 128,
                                        hs * 512:(hs + 1) * 512]
                            src = src.rearrange("(c p) h -> p c h", p=128)
                            b8 = 8192 + (Gw % 2) * 4096 + u * 2048
                            dst = STG[:, b8:b8 + 2048]
                            dst = dst.rearrange("p (c h) -> p c h", c=4)
                            s_wds[slot].inc(sync.dma_start(out=dst, in_=src),
                                            16)
                            idxd += 1
                    st["ddma"] = idxd

                blk.sync(ddma2)
            if sp_ == 2:
                def dcvt_first(vector, G=4 * phase_global):
                    for hb in range(4):
                        emit_dcvt_hb(vector, G, hb)

                blk.vector(dcvt_first)

            def tmm(tensor, gspan=gspan, buf=buf, xt_base=xt_target,
                    first=first_span_of_call, sp_=sp_,
                    phase_global=phase_global):
                tensor.wait_ge(s_cvr.h, st["cvr_at_span"][gspan])
                if sp_ == 0 and phase_global >= 1:
                    tensor.wait_ge(s_hd.h, 32 * phase_global)
                for ib in range(2):
                    nblk = st["nblk"]
                    par = nblk % 2
                    if nblk >= 2:
                        tensor.wait_ge(s_ht.h, st["ht_at_block"][nblk - 2])
                    banks = P[0:4] if par == 0 else P[4:8]
                    for mi, wsrc in enumerate((WG, WU)):
                        for g in range(NG):
                            pb = banks[mi * 2 + g]
                            for k in range(16):
                                if gspan == first and ib == 0 and \
                                        mi == 0 and g == 0:
                                    tensor.wait_ge(s_xt.h,
                                                   xt_base + k // 2 + 1)
                                lhs = wsrc[:, buf * 4096 + k * SPAN +
                                           ib * 128:buf * 4096 +
                                           k * SPAN + ib * 128 + 128]
                                rhs = XT[:, k * TOK + g * GRP:
                                         k * TOK + g * GRP + GRP]
                                ins = tensor.matmul(
                                    out=pb[:, :], lhsT=lhs, rhs=rhs,
                                    start=(k == 0), stop=(k == 15))
                    s_gu.inc(ins)
                    st["gu_at_block"].append(s_gu.n)
                    st["nblk"] += 1
                tensor.wait_ge(s_gu.h, st["gu_at_block"][-1])
                tensor.sem_inc(s_wf.h, 1)
                s_wf.n += 1

            blk.tensor(tmm)
            st["wf_at_span"].append(s_wf.n)

            def sact(scalar, gspan=gspan):
                for ib in range(2):
                    nblk = gspan * 2 + ib
                    par = nblk % 2
                    banks = P[0:4] if par == 0 else P[4:8]
                    iblk = nblk % NIB
                    scalar.wait_ge(s_gu.h, st["gu_at_block"][nblk])
                    if nblk >= 2:
                        scalar.wait_ge(s_ht.h, st["ht_at_block"][nblk - 2])
                    for g in range(NG):
                        dst = TS[:, g * 1024 + (nblk % 2) * GRP:
                                 g * 1024 + (nblk % 2) * GRP + GRP]
                        scalar.activation(
                            out=dst, in_=banks[g][:, :], func=AF.Silu,
                            scale=GS[:, iblk:iblk + 1])
                    scalar.drain()
                    scalar.sem_inc(s_t0.h, 2)
                    s_t0.n += 2
                    st["t0_at"].append(s_t0.n)

            blk.scalar(sact)

            def vmul(vector, gspan=gspan, sp_=sp_,
                     phase_global=phase_global):
                for ib in range(2):
                    nblk = gspan * 2 + ib
                    par = nblk % 2
                    banks = P[0:4] if par == 0 else P[4:8]
                    iblk = nblk % NIB
                    chunk = sp_ * 2 + ib
                    if sp_ == 0 and ib == 0 and phase_global >= 1:
                        vector.wait_ge(s_mmd.h, 32 * phase_global)
                    vector.wait_ge(s_gu.h, st["gu_at_block"][nblk])
                    vector.wait_ge(s_t0.h, st["t0_at"][nblk])
                    for g in range(NG):
                        t0 = TS[:, g * 1024 + (nblk % 2) * GRP:
                                g * 1024 + (nblk % 2) * GRP + GRP]
                        ins = vector.scalar_tensor_tensor(
                            out=HT[:, chunk * TOK + g * GRP:
                                   chunk * TOK + g * GRP + GRP],
                            in0=banks[2 + g][:, :],
                            scalar=US[:, iblk:iblk + 1], in1=t0,
                            op0=ALU.mult, op1=ALU.mult)
                        s_ht.inc(ins)
                    st["ht_at_block"].append(s_ht.n)
                    if (sp_ == 2 and ib == 1) or sp_ == 3:
                        vector.drain()
                        vector.sem_inc(s_htr.h, 1)
                        s_htr.n += 1

            blk.vector(vmul)
            st["span"] += 1

            if sp_ == NSP - 1:
                htr_fb = s_htr.n - 3

                for hs in range(4):
                    G_hs = 4 * phase_global + hs

                    def dmm_hs(tensor, hs=hs, G=G_hs, fb=htr_fb):
                        first_unit = st["dunit"]
                        for hb in range(4):
                            tensor.wait_ge(s_cvdr.h, 4 * G + hb + 1)
                            for g in range(NG):
                                kdu = st["dunit"]
                                par = kdu % 2
                                if kdu >= 2:
                                    tensor.wait_ge(s_hd.h, kdu - 1)
                                for c in range(8):
                                    if hs == 0 and kdu == first_unit:
                                        if c == 0:
                                            tensor.wait_ge(s_htr.h, fb + 1)
                                        elif c == 6:
                                            tensor.wait_ge(s_htr.h, fb + 2)
                                        elif c == 7:
                                            tensor.wait_ge(s_htr.h, fb + 3)
                                    lhs = WD[:, hb * 1024 + c * 128:
                                             hb * 1024 + c * 128 + 128]
                                    rhs = HT[:, c * TOK + g * GRP:
                                             c * TOK + g * GRP + GRP]
                                    ins = tensor.matmul(
                                        out=P[par][:, :], lhsT=lhs, rhs=rhs,
                                        start=(c == 0), stop=(c == 7))
                                s_mmd.inc(ins)
                                st["dunit"] += 1

                    blk.tensor(dmm_hs)

                    ln_flush = (phase_global % NPH == NPH - 1)

                    def dacc_hs(vector, hs=hs, G=G_hs, ln_flush=ln_flush):
                        base = st["dunit"] - 8
                        for unit in range(8):
                            kdu = base + unit
                            par = kdu % 2
                            hb, g = divmod(unit, 2)
                            h = hs * 4 + hb
                            vector.wait_ge(s_mmd.h, kdu + 1)
                            sl = slice(h * TOK + g * GRP,
                                       h * TOK + g * GRP + GRP)
                            ins = vector.scalar_tensor_tensor(
                                out=R[:, sl], in0=P[par][:, :],
                                scalar=DS2[:, h:h + 1], in1=R[:, sl],
                                op0=ALU.mult, op1=ALU.add)
                            s_hd.inc(ins)
                            if unit % 2 == 1 and hs < 3:
                                emit_dcvt_hb(vector, G + 1, unit // 2)
                        if ln_flush:
                            vector.drain()
                            vector.sem_inc(s_hdr.h, 1)
                            s_hdr.n += 1

                    blk.vector(dacc_hs)
                st["mmd_at_phase"].append(s_mmd.n)
        st["call"] += 1

    ssm_arg = [0]

    def emit_ssm_vscan():
        mmd_now = s_mmd.n

        def vscan(vector):
            vector.wait_ge(s_mmd.h, mmd_now)
            li_col = COLS4[:, 1:2]
            lr_col = COLS4[:, 0:1]
            vector.tensor_scalar(out=QI[:], in0=IOTAB[:], scalar1=li_col,
                                 scalar2=None, op0=ALU.mult)
            vector.tensor_scalar(out=Sf[:], in0=QI[:], scalar1=INV_2PI,
                                 scalar2=None, op0=ALU.mult)
            vector.tensor_copy(out=KIW[:], in_=Sf[:])
            vector.tensor_copy(out=Sf[:], in_=KIW[:])
            vector.scalar_tensor_tensor(out=Sf[:], in0=Sf[:], scalar=-TWO_PI,
                                        in1=QI[:], op0=ALU.mult, op1=ALU.add)
            vector.tensor_scalar(out=Ef[:], in0=QI[:], scalar1=HALF_PI,
                                 scalar2=None, op0=ALU.add)
            vector.tensor_scalar(out=Cf[:], in0=Ef[:], scalar1=INV_2PI,
                                 scalar2=None, op0=ALU.mult)
            vector.tensor_copy(out=KIW[:], in_=Cf[:])
            vector.tensor_copy(out=Cf[:], in_=KIW[:])
            vector.scalar_tensor_tensor(out=Cf[:], in0=Cf[:], scalar=-TWO_PI,
                                        in1=Ef[:], op0=ALU.mult, op1=ALU.add)
            vector.tensor_scalar(out=Ef[:], in0=IOTAB[:], scalar1=lr_col,
                                 scalar2=-1.0, op0=ALU.mult, op1=ALU.mult)
            vector.drain()
            vector.sem_inc(s_ve.h, 1)
            s_ve.n += 1

        blk.vector(vscan)
        ssm_arg[0] = s_ve.n

    def emit_ssm(xt_target):
        def bcw(vector, out, col_ap, band=0):
            vector.tensor_scalar(out=out, in0=ONESW[band:band + 32, :],
                                 scalar1=col_ap, scalar2=None, op0=ALU.mult)

        def cmulw(v, outr, outi, ar, ai, br, bi):
            v.tensor_tensor(out=c_t1, in0=ar, in1=br, op=ALU.mult)
            v.tensor_tensor(out=c_t2, in0=ai, in1=bi, op=ALU.mult)
            v.tensor_tensor(out=outr, in0=c_t1, in1=c_t2, op=ALU.subtract)
            v.tensor_tensor(out=c_t1, in0=ar, in1=bi, op=ALU.mult)
            v.tensor_tensor(out=c_t2, in0=ai, in1=br, op=ALU.mult)
            v.tensor_tensor(out=outi, in0=c_t1, in1=c_t2, op=ALU.add)

        hd_at_ssm = s_hd.n
        mmd_at_ssm = s_mmd.n

        def tbu(tensor):
            tensor.wait_ge(s_hd.h, hd_at_ssm)
            ins = None
            for g in range(NG):
                for k in range(NHB):
                    if g == 0:
                        tensor.wait_ge(s_xt.h, xt_target + k // 2 + 1)
                    rhs = XT[:, k * TOK + g * GRP:k * TOK + g * GRP + GRP]
                    tensor.matmul(out=P[g][0:32, :],
                                  lhsT=BBARR[:, k * 32:(k + 1) * 32],
                                  rhs=rhs, start=(k == 0),
                                  stop=(k == NHB - 1))
                for k in range(NHB):
                    rhs = XT[:, k * TOK + g * GRP:k * TOK + g * GRP + GRP]
                    ins = tensor.matmul(out=P[2 + g][0:32, :],
                                        lhsT=BBARI[:, k * 32:(k + 1) * 32],
                                        rhs=rhs, start=(k == 0),
                                        stop=(k == NHB - 1))
            s_mm.inc(ins)

        blk.tensor(tbu)
        bu_target = s_mm.n

        if ssm_arg[0] == 0:
            emit_ssm_vscan()
        arg_target = ssm_arg[0]

        def strig(scalar):
            scalar.wait_ge(s_ve.h, arg_target)
            scalar.activation(out=Ef[:], in_=Ef[:], func=AF.Exp)
            scalar.activation(out=Sf[:], in_=Sf[:], func=AF.Sin)
            scalar.activation(out=Cf[:], in_=Cf[:], func=AF.Sin)
            scalar.drain()
            scalar.sem_inc(s_ac.h, 1)
            s_ac.n += 1

        blk.scalar(strig)
        trig_target = s_ac.n

        def vscan2(vector):
            vector.wait_ge(s_ac.h, trig_target)
            vector.wait_ge(s_mm.h, bu_target)
            for g in range(NG):
                vector.tensor_copy(out=BuR[:, g * GRP:(g + 1) * GRP],
                                   in_=P[g][0:32, :])
                vector.tensor_copy(out=BuI[:, g * GRP:(g + 1) * GRP],
                                   in_=P[2 + g][0:32, :])
            vector.tensor_tensor(out=QI[:], in0=BuI[:], in1=Cf[:],
                                 op=ALU.mult)
            vector.tensor_tensor(out=CSR[:], in0=BuR[:], in1=Sf[:],
                                 op=ALU.mult)
            vector.tensor_tensor(out=QI[:], in0=QI[:], in1=CSR[:],
                                 op=ALU.subtract)
            vector.tensor_tensor(out=QI[:], in0=QI[:], in1=Ef[:],
                                 op=ALU.mult)
            vector.tensor_tensor(out=CSR[:], in0=BuI[:], in1=Sf[:],
                                 op=ALU.mult)
            vector.tensor_tensor(out=BuR[:], in0=BuR[:], in1=Cf[:],
                                 op=ALU.mult)
            vector.tensor_tensor(out=BuR[:], in0=BuR[:], in1=CSR[:],
                                 op=ALU.add)
            vector.tensor_tensor(out=BuR[:], in0=BuR[:], in1=Ef[:],
                                 op=ALU.mult)
            vector.tensor_tensor_scan(out=CSR[:], data0=SCMASK[:],
                                      data1=BuR[:], initial=ZERO32[:],
                                      op0=ALU.mult, op1=ALU.add)
            vector.tensor_tensor_scan(out=CSI[:], data0=SCMASK[:],
                                      data1=QI[:], initial=ZERO32[:],
                                      op0=ALU.mult, op1=ALU.add)
            bcw(vector, c_t3, Ef[:, T - 1:T])
            vector.reciprocal(out=c_t3, in_=c_t3)
            bcw(vector, c_t4, Cf[:, T - 1:T])
            vector.tensor_tensor(out=c_A127r, in0=c_t3, in1=c_t4,
                                 op=ALU.mult)
            bcw(vector, c_t4, Sf[:, T - 1:T])
            vector.tensor_tensor(out=c_A127i, in0=c_t3, in1=c_t4,
                                 op=ALU.mult)
            bcw(vector, c_ARw, COLS4[:, 2:3])
            bcw(vector, c_AIw, COLS4[:, 3:4])
            cmulw(vector, c_A128r, c_A128i, c_A127r, c_A127i, c_ARw, c_AIw)
            cmulw(vector, c_u1, c_u2, c_A128r, c_A128i, c_A128r, c_A128i)
            cmulw(vector, c_A10r, c_A10i, c_u1, c_u2, c_u1, c_u2)
            cmulw(vector, c_u1, c_u2, c_A10r, c_A10i, c_A10r, c_A10i)
            vector.tensor_copy(out=c_A10r, in_=c_u1)
            vector.tensor_copy(out=c_A10i, in_=c_u2)
            vector.memset(c_cwR, 0.0)
            vector.memset(c_cwI, 0.0)
            for j in range(NCH):
                sl = slice(j * T, (j + 1) * T)
                p = j * T + T - 1
                vector.tensor_copy(out=C0R[:, sl], in_=c_cwR)
                vector.tensor_copy(out=C0I[:, sl], in_=c_cwI)
                bcw(vector, c_e1, CSR[:, p:p + 1])
                bcw(vector, c_e2, CSI[:, p:p + 1])
                cmulw(vector, c_acR, c_acI, c_A127r, c_A127i, c_e1, c_e2)
                cmulw(vector, c_u1, c_u2, c_A128r, c_A128i, c_cwR, c_cwI)
                vector.tensor_tensor(out=c_cwR, in0=c_acR, in1=c_u1,
                                     op=ALU.add)
                vector.tensor_tensor(out=c_cwI, in0=c_acI, in1=c_u2,
                                     op=ALU.add)
            vector.drain()
            vector.sem_inc(s_ve.h, 1)
            s_ve.n += 1

        blk.vector(vscan2)
        scan_target = s_ve.n

        def vpre(vector):
            vector.reciprocal(out=Ef[:], in_=Ef[:])
            for j in range(NCH):
                sl = slice(j * T, (j + 1) * T)
                cmulw(vector, c_acR, c_acI, c_ARw, c_AIw,
                      C0R[:, sl], C0I[:, sl])
                vector.tensor_tensor(out=ZR[:, sl], in0=CSR[:, sl],
                                     in1=c_acR, op=ALU.add)
                vector.tensor_tensor(out=ZI[:, sl], in0=CSI[:, sl],
                                     in1=c_acI, op=ALU.add)
            vector.tensor_tensor(out=Sf[:], in0=Sf[:], in1=Ef[:],
                                 op=ALU.mult)
            vector.tensor_tensor(out=Cf[:], in0=Cf[:], in1=Ef[:],
                                 op=ALU.mult)
            vector.tensor_tensor(out=XSR[:], in0=Cf[:], in1=ZR[:],
                                 op=ALU.mult)
            vector.tensor_tensor(out=Ef[:], in0=Sf[:], in1=ZI[:],
                                 op=ALU.mult)
            vector.tensor_tensor(out=XSR[:], in0=XSR[:], in1=Ef[:],
                                 op=ALU.subtract)
            vector.tensor_tensor(out=XSI[:], in0=Cf[:], in1=ZI[:],
                                 op=ALU.mult)
            vector.tensor_tensor(out=Ef[:], in0=Sf[:], in1=ZR[:],
                                 op=ALU.mult)
            vector.tensor_tensor(out=XSI[:], in0=XSI[:], in1=Ef[:],
                                 op=ALU.add)
            vector.tensor_copy(out=c_A127r, in_=c_ARw)
            vector.tensor_copy(out=c_A127i, in_=c_AIw)
            for j in range(NCH):
                sl = slice(j * T, (j + 1) * T)
                wr, wi = ((c_A127r, c_A127i) if j % 2 == 0
                          else (c_acR, c_acI))
                nr, ni = ((c_acR, c_acI) if j % 2 == 0
                          else (c_A127r, c_A127i))
                cmulw(vector, ZR[:, sl], ZI[:, sl], Cf[:, sl], Sf[:, sl],
                      wr, wi)
                if j + 1 < NCH:
                    cmulw(vector, nr, ni, wr, wi, c_A128r, c_A128i)

        blk.vector(vpre)

        def gcoll(gpsimd):
            gpsimd.wait_ge(s_ve.h, scan_target)
            s_g.inc(gpsimd.dma_start(
                out=cc_in[0:1, 0:32].rearrange("p (a b) -> (p a) b", b=1),
                in_=c_cwR[:, 0:1]), 16)
            s_g.inc(gpsimd.dma_start(
                out=cc_in[0:1, 32:64].rearrange("p (a b) -> (p a) b", b=1),
                in_=c_cwI[:, 0:1]), 16)
            gpsimd.wait_ge(s_g.h, s_g.n)
            s_g.inc(gpsimd.collective_compute(
                "AllGather", ALU.bypass,
                replica_groups=[list(range(NCORES))],
                ins=[cc_in.opt()], outs=[cc_out.opt()]), 1)
            gpsimd.wait_ge(s_g.h, s_g.n)
            s_g.inc(gpsimd.dma_start(
                out=EG32[:],
                in_=cc_out.rearrange("r (c n) -> n (r c)", c=2),
                allow_slow_non_contiguous=True), 16)

        blk.gpsimd(gcoll)
        eg_target = s_g.n

        def vchain(vector):
            vector.wait_ge(s_g.h, eg_target)
            vector.memset(c_CwR, 0.0)
            vector.memset(c_CwI, 0.0)
            for b in range(NCORES):
                bcw(vector, c_e1, EG32[:, 2 * b:2 * b + 1])
                bcw(vector, c_e2, EG32[:, 2 * b + 1:2 * b + 2])
                if b == 0:
                    vector.tensor_copy(out=c_SR, in_=c_e1)
                    vector.tensor_copy(out=c_SI, in_=c_e2)
                else:
                    cmulw(vector, c_u1, c_u2, c_A10r, c_A10i, c_SR, c_SI)
                    vector.tensor_tensor(out=c_SR, in0=c_e1, in1=c_u1,
                                         op=ALU.add)
                    vector.tensor_tensor(out=c_SI, in0=c_e2, in1=c_u2,
                                         op=ALU.add)
                vector.scalar_tensor_tensor(out=c_CwR, in0=c_SR,
                                            scalar=SEL32[:, b:b + 1],
                                            in1=c_CwR, op0=ALU.mult,
                                            op1=ALU.add)
                vector.scalar_tensor_tensor(out=c_CwI, in0=c_SI,
                                            scalar=SEL32[:, b:b + 1],
                                            in1=c_CwI, op0=ALU.mult,
                                            op1=ALU.add)
            vector.tensor_scalar(out=c_u1, in0=c_CwI, scalar1=-1.0,
                                 scalar2=None, op0=ALU.mult)
            vector.scalar_tensor_tensor(out=XSR[:], in0=ZR[:],
                                        scalar=c_CwR[:, 0:1], in1=XSR[:],
                                        op0=ALU.mult, op1=ALU.add)
            vector.scalar_tensor_tensor(out=XSR[:], in0=ZI[:],
                                        scalar=c_u1[:, 0:1], in1=XSR[:],
                                        op0=ALU.mult, op1=ALU.add)
            vector.scalar_tensor_tensor(out=XSI[:], in0=ZR[:],
                                        scalar=c_CwI[:, 0:1], in1=XSI[:],
                                        op0=ALU.mult, op1=ALU.add)
            vector.scalar_tensor_tensor(out=XSI[:], in0=ZI[:],
                                        scalar=c_CwR[:, 0:1], in1=XSI[:],
                                        op0=ALU.mult, op1=ALU.add)
            vector.tensor_copy(out=XSB[0:32, :], in_=XSR[:])
            vector.tensor_copy(out=XSB[32:64, :], in_=XSI[:])
            vector.drain()
            vector.sem_inc(s_ve.h, 1)
            s_ve.n += 1

        blk.vector(vchain)
        xs_target = s_ve.n

        ssm_yd_base = s_yd.n
        ssm_mm_base = s_mm.n
        ssm_mm_at = []

        def sdg(sync):
            for hb in range(NHB):
                if hb >= 2:
                    sync.wait_ge(s_mm.h, ssm_mm_base + 2 * (hb - 2) + 2)
                s_dg[hb % 2].inc(sync.dma_start(
                    out=DG[:, (hb % 2) * 128:(hb % 2) * 128 + 128],
                    in_=d_dg[hb * 128:(hb + 1) * 128, :]), 16)

        blk.sync(sdg)

        def ty(tensor):
            tensor.wait_ge(s_ve.h, xs_target)
            yunit = 0
            for hb in range(NHB):
                tensor.wait_ge(s_dg[hb % 2].h, 16 * (hb // 2 + 1))
                for g in range(NG):
                    par = yunit % 2
                    if yunit >= 2:
                        tensor.wait_ge(s_yd.h, ssm_yd_base + yunit - 1)
                    tensor.matmul(
                        out=P[4 + par][:, :],
                        lhsT=C2[:, hb * 128:(hb + 1) * 128],
                        rhs=XSB[:, g * GRP:(g + 1) * GRP],
                        start=True, stop=False)
                    ins = tensor.matmul(
                        out=P[4 + par][:, :],
                        lhsT=DG[:, (hb % 2) * 128:(hb % 2) * 128 + 128],
                        rhs=XT[:, hb * TOK + g * GRP:
                               hb * TOK + g * GRP + GRP],
                        start=False, stop=True)
                    s_mm.inc(ins)
                    ssm_mm_at.append(s_mm.n)
                    yunit += 1

        blk.tensor(ty)

        def vy(vector):
            yunit = 0
            for hb in range(NHB):
                for g in range(NG):
                    par = yunit % 2
                    vector.wait_ge(s_mm.h, ssm_mm_at[yunit])
                    sl = slice(hb * TOK + g * GRP, hb * TOK + g * GRP + GRP)
                    ins = vector.tensor_tensor(out=R[:, sl], in0=R[:, sl],
                                               in1=P[4 + par][:, :],
                                               op=ALU.add)
                    s_yd.inc(ins)
                    yunit += 1
                if hb % 4 == 3:
                    vector.drain()
                    vector.sem_inc(s_hdr.h, 1)
                    s_hdr.n += 1

        blk.vector(vy)
        return xs_target

    def tap(name):
        if name not in d_taps:
            return

        def stap(sync, name=name):
            s_st.inc(sync.dma_start(
                out=d_taps[name].rearrange("(b p) t -> p b t", p=128),
                in_=R[:].rearrange("p (b t) -> p b t", b=NHB)), 16)

        blk.sync(stap)

    xt1 = emit_ln(wait_rq=lambda e, q: e.wait_ge(s_rq[q].h, 16),
                  prefetch=lambda: (emit_gu_stage(0), emit_gu_stage(1)))
    emit_ffn(xt1, n_pre=2)
    if d_taps:
        barrier()
        tap("x1")
    hb2 = s_hdr.n - 4
    xt2 = emit_ln(wait_rq=lambda e, q, b=hb2: e.wait_ge(s_hdr.h, b + q + 1))
    ssm_free = emit_ssm(xt2)
    if d_taps:
        barrier()
        tap("x2")
    hb3 = s_hdr.n - 4
    xt3 = emit_ln(wait_rq=lambda e, q, b=hb3: e.wait_ge(s_hdr.h, b + q + 1),
                  prefetch=lambda: (emit_gu_stage(32, (s_ve, ssm_free)),
                                    emit_gu_stage(33)))
    emit_ffn(xt3, n_pre=2)
    if d_taps:
        barrier()

    def store(sync):
        base = s_hdr.n - 4
        for q in range(4):
            sync.wait_ge(s_hdr.h, base + q + 1)
            dst = d_yT[q * 512:(q + 1) * 512, :].rearrange(
                "(b p) t -> p b t", p=128)
            src = R[:, q * 4 * TOK:(q + 1) * 4 * TOK].rearrange(
                "p (b t) -> p b t", b=4)
            s_st.inc(sync.dma_start(out=dst, in_=src), 16)
        sync.wait_ge(s_st.h, s_st.n)

    blk.sync(store)
    blk.__exit__(None, None, None)
    return nc


def _host_inputs(inputs, taps=()):
    asc = np.ascontiguousarray
    u = inputs["L0_attn_u"]
    def lowbyte(w):
        w = np.ascontiguousarray(np.asarray(w))
        return w.view(np.uint8).reshape(w.shape[0], w.shape[1], 4)[:, :, 0]

    wgT = asc(lowbyte(inputs["L0_ffn_gate_w"]).T)
    wuT = asc(lowbyte(inputs["L0_ffn_up_w"]).T)
    wdT = asc(lowbyte(inputs["L0_ffn_down_w"]).T)
    gsB = asc(np.asarray(inputs["L0_ffn_gate_scale"]).reshape(NIB, 128).T)
    usB = asc(np.asarray(inputs["L0_ffn_up_scale"]).reshape(NIB, 128).T)
    dsB = asc(np.asarray(inputs["L0_ffn_down_scale"]).reshape(NHB, 128).T)
    DBh = asc(np.asarray(inputs["L0_attn_D"]).reshape(NHB, 128).T)
    BreT = asc(np.asarray(inputs["L0_attn_B_re"]).T)
    BimT = asc(np.asarray(inputs["L0_attn_B_im"]).T)
    CreT = asc(np.asarray(inputs["L0_attn_C_re"]).T)
    CimT = asc(np.asarray(inputs["L0_attn_C_im"]).T)

    def padrow(v, n=RW_W):
        out = np.zeros((1, n), np.float32)
        v = np.asarray(v).reshape(-1)
        out[0, :v.size] = v
        return out

    Lu = padrow(inputs["L0_attn_Lambda_unconstrained_re"])
    Lim = padrow(inputs["L0_attn_Lambda_im"])
    logD = padrow(np.asarray(inputs["L0_attn_log_Delta"]).reshape(-1)[:32])
    iotaB = asc(np.broadcast_to((np.arange(TOK) % T).astype(BFNP),
                                (32, TOK)))
    scMask = np.ones((32, TOK), np.float32)
    scMask[:, 0::T] = 0.0
    onesW = np.ones((64, T), np.float32)
    onesB = np.ones((128, 1), BFNP)
    onesR = np.ones((1, 128), np.float32)
    Dv = np.asarray(inputs["L0_attn_D"]).reshape(-1)
    Ddiag = np.zeros((H, 128), BFNP)
    Ddiag[np.arange(H), np.arange(H) % 128] = Dv.astype(BFNP)
    in_maps = []
    for c in range(NCORES):
        xT = asc(np.asarray(u[0, c * TOK:(c + 1) * TOK, :]).T)
        sel32 = np.zeros((32, 8), np.float32)
        if c > 0:
            sel32[:, c - 1] = 1.0
        in_maps.append(dict(
            xT=xT, wgT=wgT, wuT=wuT, wdT=wdT, gsB=gsB, usB=usB, dsB=dsB,
            DB=DBh, BreT=BreT, BimT=BimT, CreT=CreT, CimT=CimT, Lu=Lu,
            Lim=Lim, logD=logD, sel32=sel32, iotaB=iotaB, scMask=scMask,
            onesW=onesW, onesB=onesB, onesR=onesR, Ddiag=Ddiag))
    return in_maps


def kernel(**inputs):
    nc = build_program(taps=())
    in_maps = _host_inputs(inputs)
    res = run_bass_kernel_spmd(nc, in_maps, core_ids=list(range(NCORES)))
    parts = [np.asarray(res.results[c]["yT"]).T for c in range(NCORES)]
    return np.concatenate(parts, axis=0).reshape(1, 8192, 2048)


# version 17
# speedup vs baseline: 1.0017x; 1.0017x over previous
import numpy as np
import ml_dtypes
import concourse.bass as bass
import concourse.mybir as mybir
from concourse.bass_utils import run_bass_kernel_spmd

BFNP = ml_dtypes.bfloat16
F32 = mybir.dt.float32
BF16 = mybir.dt.bfloat16
I32 = mybir.dt.int32
U8 = mybir.dt.uint8
AF = mybir.ActivationFunctionType
ALU = mybir.AluOpType

NCORES = 8
TOK = 1024
H = 2048
DFF = 8192
NHB = H // 128
NIB = DFF // 128
T = 128
NCH = TOK // T
NPH = 8
PHI = DFF // NPH
SPAN = 256
NSP = PHI // SPAN
GRP = 512
NG = TOK // GRP
EPS = 1e-5
TWO_PI = float(2 * np.pi)
INV_2PI = float(1.0 / (2 * np.pi))
HALF_PI = float(np.pi / 2)
RW_W = 512


class Sem:
    def __init__(self, nc, name):
        self.h = nc.alloc_semaphore(name)
        self.n = 0

    def inc(self, instr, k=1):
        instr.then_inc(self.h, k)
        self.n += k
        return self.n


def build_program(taps=()):
    nc = bass.Bass("TRN2", target_bir_lowering=False, debug=False)

    d_xT = nc.dram_tensor("xT", [H, TOK], F32, kind="ExternalInput").ap()
    d_wgT = nc.dram_tensor("wgT", [H, DFF], U8, kind="ExternalInput").ap()
    d_wuT = nc.dram_tensor("wuT", [H, DFF], U8, kind="ExternalInput").ap()
    d_wdT = nc.dram_tensor("wdT", [DFF, H], U8, kind="ExternalInput").ap()
    d_gs = nc.dram_tensor("gsB", [128, NIB], F32, kind="ExternalInput").ap()
    d_us = nc.dram_tensor("usB", [128, NIB], F32, kind="ExternalInput").ap()
    d_ds = nc.dram_tensor("dsB", [128, NHB], F32, kind="ExternalInput").ap()
    d_DB = nc.dram_tensor("DB", [128, NHB], F32, kind="ExternalInput").ap()
    d_BreT = nc.dram_tensor("BreT", [H, 32], F32, kind="ExternalInput").ap()
    d_BimT = nc.dram_tensor("BimT", [H, 32], F32, kind="ExternalInput").ap()
    d_CreT = nc.dram_tensor("CreT", [32, H], F32, kind="ExternalInput").ap()
    d_CimT = nc.dram_tensor("CimT", [32, H], F32, kind="ExternalInput").ap()
    d_Lu = nc.dram_tensor("Lu", [1, RW_W], F32, kind="ExternalInput").ap()
    d_Lim = nc.dram_tensor("Lim", [1, RW_W], F32, kind="ExternalInput").ap()
    d_logD = nc.dram_tensor("logD", [1, RW_W], F32, kind="ExternalInput").ap()
    d_sel = nc.dram_tensor("sel32", [32, 8], F32, kind="ExternalInput").ap()
    d_iotab = nc.dram_tensor("iotaB", [32, TOK], BF16,
                             kind="ExternalInput").ap()
    d_scmask = nc.dram_tensor("scMask", [32, TOK], F32,
                              kind="ExternalInput").ap()
    d_onesw = nc.dram_tensor("onesW", [64, T], F32, kind="ExternalInput").ap()
    d_ones1 = nc.dram_tensor("onesB", [128, 1], BF16,
                             kind="ExternalInput").ap()
    d_onesR = nc.dram_tensor("onesR", [1, 128], F32, kind="ExternalInput").ap()
    d_yT = nc.dram_tensor("yT", [H, TOK], F32, kind="ExternalOutput").ap()
    d_taps = {}
    for t_ in taps:
        d_taps[t_] = nc.dram_tensor("tap_" + t_, [H, TOK], F32,
                                    kind="ExternalOutput").ap()
    cc_in = nc.dram_tensor("cc_in", [1, 64], F32, kind="Internal").ap()
    cc_out = nc.dram_tensor("cc_out", [NCORES, 64], F32, kind="Internal").ap()
    d_scr = nc.dram_tensor("scrA", [1, 1024], F32, kind="Internal").ap()

    A = nc.alloc_sbuf_tensor
    R = A("R", [128, NHB * TOK], F32)
    XT = A("XT", [128, NHB * TOK], BF16)
    HT = A("HT", [128, (PHI // 128) * TOK], BF16)
    WG = A("WG", [128, 2 * 16 * SPAN], BF16)
    WU = A("WU", [128, 2 * 16 * SPAN], BF16)
    WD = A("WD", [128, (PHI // 128) * 512], BF16)
    STG = A("STG", [128, 16384], U8)
    TS = A("TS", [128, 2048], BF16)
    C2 = A("C2", [64, H], BF16)
    BBARR = A("BBARR", [128, 512], BF16)
    BBARI = A("BBARI", [128, 512], BF16)
    XSB = A("XSB", [64, TOK], BF16)
    BRE = A("BRE", [128, NHB * 32], F32)
    BIM = A("BIM", [128, NHB * 32], F32)
    IOTAB = A("IOTAB", [32, TOK], BF16)
    SCMASK = A("SCMASK", [32, TOK], F32)
    ONESW = A("ONESW", [64, T], F32)
    SEL32 = A("SEL32", [32, 8], F32)
    COLS4 = A("COLS4", [32, 4], F32)
    EG32 = A("EG32", [32, 16], F32)
    GS = A("GS", [128, NIB], F32)
    US = A("US", [128, NIB], F32)
    DS2 = A("DS2", [128, NHB], F32)
    DB = A("DBs", [128, NHB], F32)
    LUR = A("LUR", [1, RW_W], F32)
    LIMR = A("LIMR", [1, RW_W], F32)
    LOGDR = A("LOGDR", [1, RW_W], F32)
    ONE1 = A("ONE1", [128, 1], BF16)
    ONER = A("ONER", [1, 128], F32)
    EPS1 = A("EPS1", [1, 1], F32)
    NEGH = A("NEGH", [1, 1], F32)
    NEG1 = A("NEG1", [1, 1], F32)
    ONEF = A("ONEF", [1, 1], F32)
    ZERO32 = A("ZERO32", [32, 1], F32)
    DBG = A("DBG", [128, 512], F32)
    DUMV = A("DUMV", [1, 128], F32)
    DUMS = A("DUMS", [1, 128], F32)
    SCR = A("SCRROW", [1, 512], F32)

    HTF = HT[:].bitcast(F32)
    STF = STG[:].bitcast(F32)
    WDF = WD[:].bitcast(F32)
    TSF = TS[:].bitcast(F32)
    DBGI = DBG[:].bitcast(I32)

    def srow(i):
        return HTF[0:1, 2048 + i * 128:2048 + (i + 1) * 128]

    def wrow(i):
        return WDF[0:1, i * RW_W:(i + 1) * RW_W]

    r_lr, r_li, r_sp, r_dl, r_t1, r_t2, r_ar, r_ai = [srow(i)
                                                      for i in range(8)]
    r_w1, r_w2, r_fr, r_fi = [wrow(i) for i in range(4)]
    KIROW = DBGI[0:1, 0:128]

    def c128(ap):
        return ap[0:1, 0:128]

    def s32(i):
        return STF[0:32, i * TOK:(i + 1) * TOK]

    def h32(i):
        return HTF[0:32, i * TOK:(i + 1) * TOK]

    BuR, BuI, CSR, CSI = [s32(i) for i in range(4)]
    Ef, Cf, Sf, QI = [h32(i) for i in range(4)]
    ZR, ZI = BuR, QI
    XSR, XSI = CSR, CSI
    KIW = STF[0:32, 3 * TOK:4 * TOK].bitcast(I32)

    def ctileT(i):
        return TSF[0:32, i * T:(i + 1) * T]

    def ctileB(i):
        return STF[0:32, TOK + i * T:TOK + (i + 1) * T]

    c_cwR, c_cwI, c_t1, c_t2, c_t3, c_t4, c_u1, c_u2 = \
        [ctileT(i) for i in range(8)]
    c_A127r, c_A127i, c_A128r, c_A128i, c_A10r, c_A10i, c_ARw, c_AIw = \
        [ctileB(i) for i in range(8)]
    c_CwR = DBG[0:32, 0:128]
    c_CwI = DBG[0:32, 128:256]
    c_SR = DBG[0:32, 256:384]
    c_SI = DBG[0:32, 384:512]
    C0R = WDF[0:32, 0:1024]
    C0I = WDF[0:32, 1024:2048]
    c_acR, c_acI = c_SR, c_SI
    c_e1, c_e2 = c_t3, c_t4
    FRBR = TSF[0:128, 0:512]
    FIBR = TSF[0:128, 512:1024]

    P = [nc.alloc_psum_tensor(f"P{i}", [128, 512], F32) for i in range(8)]

    s_w = Sem(nc, "w")
    s_ws = [Sem(nc, f"ws{i}") for i in range(8)]
    s_wds = [Sem(nc, f"wds{i}") for i in range(4)]
    s_rq = [Sem(nc, f"rq{i}") for i in range(4)]
    s_xc = Sem(nc, "xc")
    s_hdr = Sem(nc, "hdr")
    s_cv = Sem(nc, "cv")
    s_cvr = Sem(nc, "cvr")
    s_htr = Sem(nc, "htr")
    s_cvdr = Sem(nc, "cvdr")
    s_wf = Sem(nc, "wf")
    s_cvd = Sem(nc, "cvd")
    s_gu = Sem(nc, "gu")
    s_t0 = Sem(nc, "t0")
    s_ht = Sem(nc, "ht")
    s_mmd = Sem(nc, "mmd")
    s_hd = Sem(nc, "hd")
    s_yd = Sem(nc, "yd")
    s_sq = Sem(nc, "sq")
    s_mv = Sem(nc, "mv")
    s_var = Sem(nc, "var")
    s_sqd = Sem(nc, "sqd")
    s_rst = Sem(nc, "rst")
    s_bc = Sem(nc, "bc")
    s_xt = Sem(nc, "xt")
    s_mm = Sem(nc, "mm")
    s_ve = Sem(nc, "ve")
    s_ac = Sem(nc, "ac")
    s_g = Sem(nc, "g")
    s_st = Sem(nc, "st")
    s_bar = Sem(nc, "bar")

    blk = nc.Block("main").__enter__()
    ve_init = [0]

    def barrier():
        tgt = s_bar.n + 5

        def fv(e, tgt=tgt):
            e.wait_ge(s_ve.h, ve_init[0])
            e.tensor_copy(out=DUMV[0:1, :],
                          in_=DUMV[0:1, :]).then_inc(s_bar.h, 1)
            e.wait_ge(s_bar.h, tgt)

        def fs(e, tgt=tgt):
            e.wait_ge(s_ve.h, ve_init[0])
            e.activation(out=DUMS[0:1, :], in_=DUMS[0:1, :],
                         func=AF.Identity).then_inc(s_bar.h, 1)
            e.wait_ge(s_bar.h, tgt)

        def f0(e, tgt=tgt):
            e.sem_inc(s_bar.h, 1)
            e.wait_ge(s_bar.h, tgt)

        blk.sync(f0)
        blk.tensor(f0)
        blk.vector(fv)
        blk.scalar(fs)
        blk.gpsimd(f0)
        s_bar.n = tgt

    def s0(sync):
        for q in range(4):
            dst = R[:, q * 4 * TOK:(q + 1) * 4 * TOK].rearrange(
                "p (b t) -> p b t", b=4)
            src = d_xT[q * 512:(q + 1) * 512, :].rearrange(
                "(b p) t -> p b t", p=128)
            s_rq[q].inc(sync.dma_start(out=dst, in_=src), 16)
        loads = [
            (GS[:], d_gs), (US[:], d_us), (DB[:], d_DB), (DS2[:], d_ds),
            (BRE[:].rearrange("p (b n) -> p b n", n=32),
             d_BreT.rearrange("(b p) n -> p b n", p=128)),
            (BIM[:].rearrange("p (b n) -> p b n", n=32),
             d_BimT.rearrange("(b p) n -> p b n", p=128)),
            (HTF[0:32, 0:2048], d_CreT),
            (HTF[0:32, 2048:4096], d_CimT),
            (LUR[:], d_Lu), (LIMR[:], d_Lim), (LOGDR[:], d_logD),
            (SEL32[:], d_sel), (IOTAB[:], d_iotab), (SCMASK[:], d_scmask),
            (ONESW[:], d_onesw), (ONE1[:], d_ones1), (ONER[:], d_onesR),
        ]
        for dst, src in loads:
            s_w.inc(sync.dma_start(out=dst, in_=src), 16)

    blk.sync(s0)
    n_loads_done = s_w.n

    def s0v(vector):
        vector.memset(EPS1[:], EPS)
        vector.memset(NEGH[:], -0.5)
        vector.memset(NEG1[:], -1.0)
        vector.memset(ONEF[:], 1.0)
        vector.memset(ZERO32[:], 0.0)
        vector.memset(DUMV[:], 0.0)
        vector.wait_ge(s_w.h, n_loads_done)
        vector.tensor_scalar(out=DS2[:], in0=DS2[:], scalar1=0.5,
                             scalar2=None, op0=ALU.mult)
        vector.tensor_scalar(out=C2[0:32, :], in0=HTF[0:32, 0:2048],
                             scalar1=2.0, scalar2=None, op0=ALU.mult)
        vector.tensor_scalar(out=C2[32:64, :], in0=HTF[0:32, 2048:4096],
                             scalar1=-2.0, scalar2=None, op0=ALU.mult)
        s_ve.inc(vector.memset(DUMS[:], 0.0))

    blk.vector(s0v)
    ve_init[0] = s_ve.n
    barrier()

    def prep_s1(scalar):
        scalar.activation(out=r_t1, in_=c128(LUR[:]), func=AF.Abs)
        scalar.activation(out=r_t2, in_=r_t1, func=AF.Exp, scale=NEG1[0:1, :])
        scalar.activation(out=r_t2, in_=r_t2, func=AF.Ln,
                          bias=ONEF[0:1, :])
        scalar.activation(out=r_dl, in_=c128(LOGDR[:]), func=AF.Exp)
        scalar.drain()
        scalar.sem_inc(s_ac.h, 2)
        s_ac.n += 2

    blk.scalar(prep_s1)

    def wrap_row(vector, arg, out, ktmp):
        vector.tensor_scalar(out=out, in0=arg, scalar1=INV_2PI, scalar2=None,
                             op0=ALU.mult)
        vector.tensor_copy(out=ktmp, in_=out)
        vector.tensor_copy(out=out, in_=ktmp)
        vector.scalar_tensor_tensor(out=out, in0=out, scalar=-TWO_PI, in1=arg,
                                    op0=ALU.mult, op1=ALU.add)

    def prep_v1(vector):
        vector.wait_ge(s_ac.h, s_ac.n)
        vector.tensor_scalar(out=r_sp, in0=c128(LUR[:]), scalar1=0.0,
                             scalar2=None, op0=ALU.max)
        vector.tensor_tensor(out=r_sp, in0=r_sp, in1=r_t2, op=ALU.add)
        vector.tensor_scalar(out=r_t1, in0=r_sp, scalar1=0.0101, scalar2=-1.0,
                             op0=ALU.add, op1=ALU.mult)
        vector.tensor_tensor(out=r_lr, in0=r_t1, in1=r_dl, op=ALU.mult)
        vector.tensor_tensor(out=r_li, in0=c128(LIMR[:]), in1=r_dl,
                             op=ALU.mult)
        vector.tensor_tensor(out=c128(r_w1), in0=r_t1, in1=r_t1, op=ALU.mult)
        vector.tensor_tensor(out=c128(r_w2), in0=c128(LIMR[:]),
                             in1=c128(LIMR[:]), op=ALU.mult)
        vector.tensor_tensor(out=r_sp, in0=c128(r_w1), in1=c128(r_w2),
                             op=ALU.add)
        vector.reciprocal(out=r_sp, in_=r_sp)
        wrap_row(vector, r_li, c128(r_w1), KIROW)
        vector.tensor_scalar(out=r_t2, in0=r_li, scalar1=HALF_PI,
                             scalar2=None, op0=ALU.add)
        wrap_row(vector, r_t2, c128(r_w2), KIROW)
        vector.drain()
        vector.sem_inc(s_ve.h, 1)
        s_ve.n += 1

    blk.vector(prep_v1)

    def prep_s2(scalar):
        scalar.wait_ge(s_ve.h, s_ve.n)
        scalar.activation(out=r_ar, in_=r_lr, func=AF.Exp)
        scalar.activation(out=c128(r_w1), in_=c128(r_w1),
                          func=AF.Sin)
        scalar.activation(out=c128(r_w2), in_=c128(r_w2),
                          func=AF.Sin)
        scalar.drain()
        scalar.sem_inc(s_ac.h, 1)
        s_ac.n += 1

    blk.scalar(prep_s2)

    def prep_v2(vector):
        vector.wait_ge(s_ac.h, s_ac.n)
        vector.tensor_tensor(out=r_ai, in0=r_ar, in1=c128(r_w1), op=ALU.mult)
        vector.tensor_tensor(out=r_ar, in0=r_ar, in1=c128(r_w2), op=ALU.mult)
        vector.tensor_scalar(out=c128(r_w1), in0=r_ar, scalar1=-1.0,
                             scalar2=None, op0=ALU.add)
        vector.tensor_tensor(out=c128(r_fr), in0=c128(r_w1), in1=r_t1,
                             op=ALU.mult)
        vector.tensor_tensor(out=c128(r_w2), in0=r_ai, in1=c128(LIMR[:]),
                             op=ALU.mult)
        vector.tensor_tensor(out=c128(r_fr), in0=c128(r_fr), in1=c128(r_w2),
                             op=ALU.add)
        vector.tensor_tensor(out=c128(r_fr), in0=c128(r_fr), in1=r_sp,
                             op=ALU.mult)
        vector.tensor_tensor(out=c128(r_fi), in0=r_ai, in1=r_t1, op=ALU.mult)
        vector.tensor_tensor(out=c128(r_w2), in0=c128(r_w1), in1=c128(LIMR[:]),
                             op=ALU.mult)
        vector.tensor_tensor(out=c128(r_fi), in0=c128(r_fi), in1=c128(r_w2),
                             op=ALU.subtract)
        vector.tensor_tensor(out=c128(r_fi), in0=c128(r_fi), in1=r_sp,
                             op=ALU.mult)
        vector.drain()
        vector.sem_inc(s_ve.h, 1)
        s_ve.n += 1

    blk.vector(prep_v2)
    prep_rows_done = s_ve.n

    def prep_g1(gpsimd):
        gpsimd.wait_ge(s_ve.h, prep_rows_done)
        for q, rowap in enumerate((r_lr, r_li, r_ar, r_ai)):
            s_g.inc(gpsimd.dma_start(out=d_scr[0:1, q * 32:(q + 1) * 32],
                                     in_=rowap[0:1, 0:32]), 16)
        gpsimd.wait_ge(s_g.h, s_g.n)
        s_g.inc(gpsimd.dma_start(
            out=COLS4[:],
            in_=d_scr[0:1, 0:128].rearrange("p (q n) -> n (p q)", n=32),
            allow_slow_non_contiguous=True), 16)
        s_g.inc(gpsimd.dma_start(out=d_scr[0:1, 128:160],
                                 in_=r_fr[0:1, 0:32]), 16)
        gpsimd.wait_ge(s_g.h, s_g.n)
        w = 32
        while w < 512:
            s_g.inc(gpsimd.dma_start(out=d_scr[0:1, 128 + w:128 + 2 * w],
                                     in_=d_scr[0:1, 128:128 + w]), 16)
            gpsimd.wait_ge(s_g.h, s_g.n)
            w *= 2
        s_g.inc(gpsimd.dma_start(out=SCR[0:1, 0:512],
                                 in_=d_scr[0:1, 128:640]), 16)
        gpsimd.wait_ge(s_g.h, s_g.n)
        s_g.inc(gpsimd.dma_start(out=d_scr[0:1, 128:160],
                                 in_=r_fi[0:1, 0:32]), 16)
        gpsimd.wait_ge(s_g.h, s_g.n)
        w = 32
        while w < 512:
            s_g.inc(gpsimd.dma_start(out=d_scr[0:1, 128 + w:128 + 2 * w],
                                     in_=d_scr[0:1, 128:128 + w]), 16)
            gpsimd.wait_ge(s_g.h, s_g.n)
            w *= 2
        s_g.inc(gpsimd.dma_start(out=r_w1[0:1, 0:512],
                                 in_=d_scr[0:1, 128:640]), 16)
        gpsimd.wait_ge(s_g.h, s_g.n)

    blk.gpsimd(prep_g1)
    prep_cols_done = s_g.n

    def prep_t1(tensor):
        tensor.wait_ge(s_g.h, prep_cols_done)
        tensor.matmul(out=P[2][:, :], lhsT=ONER[:], rhs=SCR[0:1, 0:512],
                      start=True, stop=True)
        s_mm.inc(tensor.matmul(out=P[3][:, :], lhsT=ONER[:],
                               rhs=r_w1[0:1, 0:512], start=True, stop=True))

    blk.tensor(prep_t1)
    prep_mm_done = s_mm.n

    def prep_v3(vector):
        vector.wait_ge(s_mm.h, prep_mm_done)
        vector.tensor_tensor(out=DBG[:], in0=P[2][:, :], in1=BRE[:],
                             op=ALU.mult)
        vector.tensor_tensor(out=BBARR[:], in0=P[3][:, :], in1=BIM[:],
                             op=ALU.mult)
        vector.tensor_tensor(out=BBARR[:], in0=DBG[:], in1=BBARR[:],
                             op=ALU.subtract)
        vector.tensor_tensor(out=DBG[:], in0=P[2][:, :], in1=BIM[:],
                             op=ALU.mult)
        vector.tensor_tensor(out=BBARI[:], in0=P[3][:, :], in1=BRE[:],
                             op=ALU.mult)
        vector.tensor_tensor(out=BBARI[:], in0=DBG[:], in1=BBARI[:],
                             op=ALU.add)
        vector.drain()
        vector.sem_inc(s_ve.h, 1)
        s_ve.n += 1

    blk.vector(prep_v3)
    prep_done = s_ve.n

    def emit_ln(wait_rq=None, prefetch=None):
        mrow = HTF[0:1, 0:1024]
        vrow = HTF[0:1, 1024:2048]
        xc_base = s_xc.n

        def vcopy(vector):
            for q in range(4):
                if wait_rq is not None:
                    wait_rq(vector, q)
                for b in range(q * 4, q * 4 + 4):
                    vector.tensor_copy(out=XT[:, b * TOK:(b + 1) * TOK],
                                       in_=R[:, b * TOK:(b + 1) * TOK])
                vector.drain()
                vector.sem_inc(s_xc.h, 1)
                s_xc.n += 1

        blk.vector(vcopy)
        if prefetch is not None:
            prefetch()
        sq_base = s_sq.n
        mv_base = s_mv.n

        def scopy(scalar):
            for b in range(NHB):
                if b % 4 == 0 and wait_rq is not None:
                    wait_rq(scalar, b // 4)
                if b >= 2:
                    scalar.wait_ge(s_mv.h, mv_base + (b - 1))
                scalar.activation(
                    out=TS[:, (b % 2) * 1024:(b % 2) * 1024 + 1024],
                    in_=R[:, b * TOK:(b + 1) * TOK], func=AF.Square)
                scalar.drain()
                scalar.sem_inc(s_sq.h, 1)
                s_sq.n += 1

        blk.scalar(scopy)
        psum_guards = [(s_ht, s_ht.n), (s_t0, s_t0.n),
                       (s_hd, s_hd.n), (s_yd, s_yd.n)]

        def tstats(tensor):
            for sem, val in psum_guards:
                if val:
                    tensor.wait_ge(sem.h, val)
            for b in range(NHB):
                if b % 4 == 0:
                    tensor.wait_ge(s_xc.h, xc_base + b // 4 + 1)
                for g in range(NG):
                    tensor.matmul(
                        out=P[4 + g][0:1, :], lhsT=ONE1[:],
                        rhs=XT[:, b * TOK + g * GRP:b * TOK + g * GRP + GRP],
                        start=(b == 0), stop=(b == NHB - 1))
            for b in range(NHB):
                tensor.wait_ge(s_sq.h, sq_base + b + 1)
                last = (b == NHB - 1)
                tensor.matmul(
                    out=P[6][0:1, :], lhsT=ONE1[:],
                    rhs=TS[:, (b % 2) * 1024:(b % 2) * 1024 + GRP],
                    start=(b == 0), stop=last)
                s_mv.inc(tensor.matmul(
                    out=P[7][0:1, :], lhsT=ONE1[:],
                    rhs=TS[:, (b % 2) * 1024 + GRP:(b % 2) * 1024 + 2 * GRP],
                    start=(b == 0), stop=last))

        blk.tensor(tstats)
        stat_target = s_mv.n

        def vrows(vector):
            vector.wait_ge(s_mv.h, stat_target)
            for g in range(NG):
                m = mrow[0:1, g * GRP:(g + 1) * GRP]
                v = vrow[0:1, g * GRP:(g + 1) * GRP]
                vector.tensor_scalar(out=m, in0=P[4 + g][0:1, :],
                                     scalar1=1.0 / H, scalar2=None,
                                     op0=ALU.mult)
                vector.tensor_tensor(out=SCR[0:1, 0:GRP], in0=m, in1=m,
                                     op=ALU.mult)
                vector.scalar_tensor_tensor(out=v, in0=P[6 + g][0:1, :],
                                            scalar=1.0 / H,
                                            in1=SCR[0:1, 0:GRP],
                                            op0=ALU.mult, op1=ALU.subtract)
            vector.drain()
            vector.sem_inc(s_var.h, 1)
            s_var.n += 1

        blk.vector(vrows)
        var_target = s_var.n

        def ssqrt(scalar):
            scalar.wait_ge(s_var.h, var_target)
            scalar.activation(out=vrow[0:1, 0:1024], in_=vrow[0:1, 0:1024],
                              func=AF.Ln, bias=EPS1[0:1, :])
            scalar.activation(out=vrow[0:1, 0:1024], in_=vrow[0:1, 0:1024],
                              func=AF.Exp, scale=NEGH[0:1, :])
            scalar.drain()
            scalar.sem_inc(s_sqd.h, 1)
            s_sqd.n += 1

        blk.scalar(ssqrt)
        sqd_target = s_sqd.n

        def tbcast(tensor):
            tensor.wait_ge(s_sqd.h, sqd_target)
            for g in range(NG):
                tensor.matmul(out=P[4 + g][:, :], lhsT=ONER[:],
                              rhs=mrow[0:1, g * GRP:(g + 1) * GRP],
                              start=True, stop=True)
            for g in range(NG):
                ins = tensor.matmul(out=P[6 + g][:, :], lhsT=ONER[:],
                                    rhs=vrow[0:1, g * GRP:(g + 1) * GRP],
                                    start=True, stop=True)
            s_bc.inc(ins)

        blk.tensor(tbcast)
        bc_target = s_bc.n

        def vnorm(vector):
            vector.wait_ge(s_bc.h, bc_target)
            for b in range(NHB):
                for g in range(NG):
                    sl = slice(b * TOK + g * GRP, b * TOK + g * GRP + GRP)
                    vector.tensor_tensor(out=XT[:, sl], in0=R[:, sl],
                                         in1=P[4 + g][:, :], op=ALU.subtract)
                    vector.tensor_tensor(out=XT[:, sl], in0=XT[:, sl],
                                         in1=P[6 + g][:, :], op=ALU.mult)
                if b % 2 == 1:
                    vector.drain()
                    vector.sem_inc(s_xt.h, 1)
                    s_xt.n += 1

        xt_base = s_xt.n
        blk.vector(vnorm)
        return xt_base

    st = dict(gu_dma=0, span=0, nblk=0, ddma=0, dcvt=0, dhs=0, dunit=0,
              wf_at_span=[], cv_at_span=[], gu_at_block=[], t0_at=[],
              ht_at_block=[], cvd_at_hs=[], mmd_at_phase=[], call=0,
              pending_vmul=None)

    def emit_gu_stage(gspan, extra=None):
        local_s = gspan % (NPH * NSP)
        i0 = (local_s // NSP) * PHI + (local_s % NSP) * SPAN
        buf = gspan % 2

        def sdma(sync):
            if extra is not None:
                sync.wait_ge(extra[0].h, extra[1])
            for dmat in (d_wgT, d_wuT):
                for u in range(4):
                    idx = st["gu_dma"]
                    slot = idx % 8
                    if idx >= 8:
                        sync.wait_ge(s_cv.h, idx - 8 + 1)
                    src = dmat[4 * u * 128:(4 * u + 4) * 128,
                               i0:i0 + SPAN]
                    src = src.rearrange("(c p) i -> p c i", p=128)
                    dst = STG[:, slot * 1024:(slot + 1) * 1024]
                    dst = dst.rearrange("p (c i) -> p c i", c=4)
                    s_ws[slot].inc(sync.dma_start(out=dst, in_=src), 16)
                    st["gu_dma"] += 1

        blk.sync(sdma)

        def gcvt(vector):
            for mi, wdst in enumerate((WG, WU)):
                for u in range(4):
                    idx = gspan * 8 + mi * 4 + u
                    slot = idx % 8
                    vector.wait_ge(s_ws[slot].h, 16 * (idx // 8 + 1))
                    if gspan >= 2 and mi == 0 and u == 0:
                        vector.wait_ge(s_wf.h,
                                       st["wf_at_span"][gspan - 2])
                    src = STG[:, slot * 1024:(slot + 1) * 1024]
                    dst = wdst[:, buf * 4096 + u * 1024:
                               buf * 4096 + (u + 1) * 1024]
                    s_cv.inc(vector.tensor_scalar(
                        out=dst, in0=src, scalar1=-128, scalar2=None,
                        op0=ALU.add))
            vector.drain()
            vector.sem_inc(s_cvr.h, 1)
            s_cvr.n += 1

        blk.vector(gcvt)
        st["cv_at_span"].append(s_cv.n)
        st.setdefault("cvr_at_span", []).append(s_cvr.n)

    def emit_ffn(xt_target, first_dma_extra_wait=None, n_pre=0):
        first_span_of_call = st["span"]
        nspans = NPH * NSP

        def emit_dcvt_hb(vector, G, hb):
            if hb == 0:
                for u in range(2):
                    vector.wait_ge(s_wds[(G % 2) * 2 + u].h,
                                   16 * (G // 2 + 1))
            if G >= 1:
                vector.wait_ge(s_mmd.h, 8 * (G - 1) + hb * 2 + 2)
            for u in range(2):
                base8 = 8192 + (G % 2) * 4096 + u * 2048
                src = STG[:, base8:base8 + 2048]
                src = src.rearrange("p (c h) -> p c h", h=512)
                s_cvd.inc(vector.tensor_scalar(
                    out=WD[:, hb * 1024 + u * 512:
                           hb * 1024 + u * 512 + 512],
                    in0=src[:, :, hb * 128:(hb + 1) * 128],
                    scalar1=-128, scalar2=None, op0=ALU.add))
            vector.drain()
            vector.sem_inc(s_cvdr.h, 1)
            s_cvdr.n += 1

        if n_pre == 0:
            emit_gu_stage(first_span_of_call, extra=first_dma_extra_wait)
            n_pre = 1
        for local_s in range(nspans):
            gspan = st["span"]
            ph = local_s // NSP
            sp_ = local_s % NSP
            phase_global = len(st["mmd_at_phase"])
            buf = gspan % 2
            if n_pre <= local_s + 1 < nspans:
                emit_gu_stage(first_span_of_call + local_s + 1)
            if sp_ == 1 or sp_ == NSP - 1:
                hs_pair = (0, 1) if sp_ == 1 else (2, 3)

                def ddma2(sync, ph=ph, hs_pair=hs_pair, base=st["ddma"]):
                    idxd = base
                    for hs in hs_pair:
                        for u in range(2):
                            Gw = idxd // 2
                            slot = (Gw % 2) * 2 + u
                            if Gw >= 2:
                                sync.wait_ge(s_cvd.h, 8 * (Gw - 1))
                            src = d_wdT[ph * PHI + 4 * u * 128:
                                        ph * PHI + (4 * u + 4) * 128,
                                        hs * 512:(hs + 1) * 512]
                            src = src.rearrange("(c p) h -> p c h", p=128)
                            b8 = 8192 + (Gw % 2) * 4096 + u * 2048
                            dst = STG[:, b8:b8 + 2048]
                            dst = dst.rearrange("p (c h) -> p c h", c=4)
                            s_wds[slot].inc(sync.dma_start(out=dst, in_=src),
                                            16)
                            idxd += 1
                    st["ddma"] = idxd

                blk.sync(ddma2)
            if sp_ == 2:
                def dcvt_first(vector, G=4 * phase_global):
                    for hb in range(4):
                        emit_dcvt_hb(vector, G, hb)

                blk.vector(dcvt_first)

            def tmm(tensor, gspan=gspan, buf=buf, xt_base=xt_target,
                    first=first_span_of_call, sp_=sp_,
                    phase_global=phase_global):
                tensor.wait_ge(s_cvr.h, st["cvr_at_span"][gspan])
                if sp_ == 0 and phase_global >= 1:
                    tensor.wait_ge(s_hd.h, 32 * phase_global)
                for ib in range(2):
                    nblk = st["nblk"]
                    par = nblk % 2
                    if nblk >= 2:
                        tensor.wait_ge(s_ht.h, st["ht_at_block"][nblk - 2])
                    banks = P[0:4] if par == 0 else P[4:8]
                    for mi, wsrc in enumerate((WG, WU)):
                        for g in range(NG):
                            pb = banks[mi * 2 + g]
                            for k in range(16):
                                if gspan == first and ib == 0 and \
                                        mi == 0 and g == 0:
                                    tensor.wait_ge(s_xt.h,
                                                   xt_base + k // 2 + 1)
                                lhs = wsrc[:, buf * 4096 + k * SPAN +
                                           ib * 128:buf * 4096 +
                                           k * SPAN + ib * 128 + 128]
                                rhs = XT[:, k * TOK + g * GRP:
                                         k * TOK + g * GRP + GRP]
                                ins = tensor.matmul(
                                    out=pb[:, :], lhsT=lhs, rhs=rhs,
                                    start=(k == 0), stop=(k == 15))
                    s_gu.inc(ins)
                    st["gu_at_block"].append(s_gu.n)
                    st["nblk"] += 1
                tensor.wait_ge(s_gu.h, st["gu_at_block"][-1])
                tensor.sem_inc(s_wf.h, 1)
                s_wf.n += 1

            blk.tensor(tmm)
            st["wf_at_span"].append(s_wf.n)

            def sact(scalar, gspan=gspan):
                for ib in range(2):
                    nblk = gspan * 2 + ib
                    par = nblk % 2
                    banks = P[0:4] if par == 0 else P[4:8]
                    iblk = nblk % NIB
                    scalar.wait_ge(s_gu.h, st["gu_at_block"][nblk])
                    if nblk >= 2:
                        scalar.wait_ge(s_ht.h, st["ht_at_block"][nblk - 2])
                    for g in range(NG):
                        dst = TS[:, g * 1024 + (nblk % 2) * GRP:
                                 g * 1024 + (nblk % 2) * GRP + GRP]
                        scalar.activation(
                            out=dst, in_=banks[g][:, :], func=AF.Silu,
                            scale=GS[:, iblk:iblk + 1])
                    scalar.drain()
                    scalar.sem_inc(s_t0.h, 2)
                    s_t0.n += 2
                    st["t0_at"].append(s_t0.n)

            blk.scalar(sact)

            def vmul(vector, gspan=gspan, sp_=sp_,
                     phase_global=phase_global):
                for ib in range(2):
                    nblk = gspan * 2 + ib
                    par = nblk % 2
                    banks = P[0:4] if par == 0 else P[4:8]
                    iblk = nblk % NIB
                    chunk = sp_ * 2 + ib
                    if sp_ == 0 and ib == 0 and phase_global >= 1:
                        vector.wait_ge(s_mmd.h, 32 * phase_global)
                    vector.wait_ge(s_gu.h, st["gu_at_block"][nblk])
                    vector.wait_ge(s_t0.h, st["t0_at"][nblk])
                    for g in range(NG):
                        t0 = TS[:, g * 1024 + (nblk % 2) * GRP:
                                g * 1024 + (nblk % 2) * GRP + GRP]
                        ins = vector.scalar_tensor_tensor(
                            out=HT[:, chunk * TOK + g * GRP:
                                   chunk * TOK + g * GRP + GRP],
                            in0=banks[2 + g][:, :],
                            scalar=US[:, iblk:iblk + 1], in1=t0,
                            op0=ALU.mult, op1=ALU.mult)
                        s_ht.inc(ins)
                    st["ht_at_block"].append(s_ht.n)
                    if (sp_ == 2 and ib == 1) or sp_ == 3:
                        vector.drain()
                        vector.sem_inc(s_htr.h, 1)
                        s_htr.n += 1

            blk.vector(vmul)
            st["span"] += 1

            if sp_ == NSP - 1:
                htr_fb = s_htr.n - 3

                for hs in range(4):
                    G_hs = 4 * phase_global + hs

                    def dmm_hs(tensor, hs=hs, G=G_hs, fb=htr_fb):
                        first_unit = st["dunit"]
                        for hb in range(4):
                            tensor.wait_ge(s_cvdr.h, 4 * G + hb + 1)
                            for g in range(NG):
                                kdu = st["dunit"]
                                par = kdu % 4
                                if kdu >= 4:
                                    tensor.wait_ge(s_hd.h, kdu - 3)
                                for c in range(8):
                                    if hs == 0 and kdu == first_unit:
                                        if c == 0:
                                            tensor.wait_ge(s_htr.h, fb + 1)
                                        elif c == 6:
                                            tensor.wait_ge(s_htr.h, fb + 2)
                                        elif c == 7:
                                            tensor.wait_ge(s_htr.h, fb + 3)
                                    lhs = WD[:, hb * 1024 + c * 128:
                                             hb * 1024 + c * 128 + 128]
                                    rhs = HT[:, c * TOK + g * GRP:
                                             c * TOK + g * GRP + GRP]
                                    ins = tensor.matmul(
                                        out=P[par][:, :], lhsT=lhs, rhs=rhs,
                                        start=(c == 0), stop=(c == 7))
                                s_mmd.inc(ins)
                                st["dunit"] += 1

                    blk.tensor(dmm_hs)

                    ln_flush = (phase_global % NPH == NPH - 1)

                    def dacc_hs(vector, hs=hs, G=G_hs, ln_flush=ln_flush):
                        base = st["dunit"] - 8
                        for unit in range(8):
                            kdu = base + unit
                            par = kdu % 4
                            hb, g = divmod(unit, 2)
                            h = hs * 4 + hb
                            vector.wait_ge(s_mmd.h, kdu + 1)
                            sl = slice(h * TOK + g * GRP,
                                       h * TOK + g * GRP + GRP)
                            ins = vector.scalar_tensor_tensor(
                                out=R[:, sl], in0=P[par][:, :],
                                scalar=DS2[:, h:h + 1], in1=R[:, sl],
                                op0=ALU.mult, op1=ALU.add)
                            s_hd.inc(ins)
                            if unit % 2 == 1 and hs < 3:
                                emit_dcvt_hb(vector, G + 1, unit // 2)
                        if ln_flush:
                            vector.drain()
                            vector.sem_inc(s_hdr.h, 1)
                            s_hdr.n += 1

                    blk.vector(dacc_hs)
                st["mmd_at_phase"].append(s_mmd.n)
        st["call"] += 1

    ssm_arg = [0]

    def emit_ssm_vscan():
        mmd_now = s_mmd.n

        def vscan(vector):
            vector.wait_ge(s_mmd.h, mmd_now)
            li_col = COLS4[:, 1:2]
            lr_col = COLS4[:, 0:1]
            vector.tensor_scalar(out=QI[:], in0=IOTAB[:], scalar1=li_col,
                                 scalar2=None, op0=ALU.mult)
            vector.tensor_scalar(out=Sf[:], in0=QI[:], scalar1=INV_2PI,
                                 scalar2=None, op0=ALU.mult)
            vector.tensor_copy(out=KIW[:], in_=Sf[:])
            vector.tensor_copy(out=Sf[:], in_=KIW[:])
            vector.scalar_tensor_tensor(out=Sf[:], in0=Sf[:], scalar=-TWO_PI,
                                        in1=QI[:], op0=ALU.mult, op1=ALU.add)
            vector.tensor_scalar(out=Ef[:], in0=QI[:], scalar1=HALF_PI,
                                 scalar2=None, op0=ALU.add)
            vector.tensor_scalar(out=Cf[:], in0=Ef[:], scalar1=INV_2PI,
                                 scalar2=None, op0=ALU.mult)
            vector.tensor_copy(out=KIW[:], in_=Cf[:])
            vector.tensor_copy(out=Cf[:], in_=KIW[:])
            vector.scalar_tensor_tensor(out=Cf[:], in0=Cf[:], scalar=-TWO_PI,
                                        in1=Ef[:], op0=ALU.mult, op1=ALU.add)
            vector.tensor_scalar(out=Ef[:], in0=IOTAB[:], scalar1=lr_col,
                                 scalar2=-1.0, op0=ALU.mult, op1=ALU.mult)
            vector.drain()
            vector.sem_inc(s_ve.h, 1)
            s_ve.n += 1

        blk.vector(vscan)
        ssm_arg[0] = s_ve.n

    def emit_ssm(xt_target):
        def bcw(vector, out, col_ap, band=0):
            vector.tensor_scalar(out=out, in0=ONESW[band:band + 32, :],
                                 scalar1=col_ap, scalar2=None, op0=ALU.mult)

        def cmulw(v, outr, outi, ar, ai, br, bi):
            v.tensor_tensor(out=c_t1, in0=ar, in1=br, op=ALU.mult)
            v.tensor_tensor(out=c_t2, in0=ai, in1=bi, op=ALU.mult)
            v.tensor_tensor(out=outr, in0=c_t1, in1=c_t2, op=ALU.subtract)
            v.tensor_tensor(out=c_t1, in0=ar, in1=bi, op=ALU.mult)
            v.tensor_tensor(out=c_t2, in0=ai, in1=br, op=ALU.mult)
            v.tensor_tensor(out=outi, in0=c_t1, in1=c_t2, op=ALU.add)

        hd_at_ssm = s_hd.n
        mmd_at_ssm = s_mmd.n

        def tbu(tensor):
            tensor.wait_ge(s_hd.h, hd_at_ssm)
            ins = None
            for g in range(NG):
                for k in range(NHB):
                    if g == 0:
                        tensor.wait_ge(s_xt.h, xt_target + k // 2 + 1)
                    rhs = XT[:, k * TOK + g * GRP:k * TOK + g * GRP + GRP]
                    tensor.matmul(out=P[g][0:32, :],
                                  lhsT=BBARR[:, k * 32:(k + 1) * 32],
                                  rhs=rhs, start=(k == 0),
                                  stop=(k == NHB - 1))
                for k in range(NHB):
                    rhs = XT[:, k * TOK + g * GRP:k * TOK + g * GRP + GRP]
                    ins = tensor.matmul(out=P[2 + g][0:32, :],
                                        lhsT=BBARI[:, k * 32:(k + 1) * 32],
                                        rhs=rhs, start=(k == 0),
                                        stop=(k == NHB - 1))
            s_mm.inc(ins)

        blk.tensor(tbu)
        bu_target = s_mm.n

        if ssm_arg[0] == 0:
            emit_ssm_vscan()
        arg_target = ssm_arg[0]

        def strig(scalar):
            scalar.wait_ge(s_ve.h, arg_target)
            scalar.activation(out=Ef[:], in_=Ef[:], func=AF.Exp)
            scalar.activation(out=Sf[:], in_=Sf[:], func=AF.Sin)
            scalar.activation(out=Cf[:], in_=Cf[:], func=AF.Sin)
            scalar.drain()
            scalar.sem_inc(s_ac.h, 1)
            s_ac.n += 1

        blk.scalar(strig)
        trig_target = s_ac.n

        def vscan2(vector):
            vector.wait_ge(s_ac.h, trig_target)
            vector.wait_ge(s_mm.h, bu_target)
            for g in range(NG):
                vector.tensor_copy(out=BuR[:, g * GRP:(g + 1) * GRP],
                                   in_=P[g][0:32, :])
                vector.tensor_copy(out=BuI[:, g * GRP:(g + 1) * GRP],
                                   in_=P[2 + g][0:32, :])
            vector.tensor_tensor(out=QI[:], in0=BuI[:], in1=Cf[:],
                                 op=ALU.mult)
            vector.tensor_tensor(out=CSR[:], in0=BuR[:], in1=Sf[:],
                                 op=ALU.mult)
            vector.tensor_tensor(out=QI[:], in0=QI[:], in1=CSR[:],
                                 op=ALU.subtract)
            vector.tensor_tensor(out=QI[:], in0=QI[:], in1=Ef[:],
                                 op=ALU.mult)
            vector.tensor_tensor(out=CSR[:], in0=BuI[:], in1=Sf[:],
                                 op=ALU.mult)
            vector.tensor_tensor(out=BuR[:], in0=BuR[:], in1=Cf[:],
                                 op=ALU.mult)
            vector.tensor_tensor(out=BuR[:], in0=BuR[:], in1=CSR[:],
                                 op=ALU.add)
            vector.tensor_tensor(out=BuR[:], in0=BuR[:], in1=Ef[:],
                                 op=ALU.mult)
            vector.tensor_tensor_scan(out=CSR[:], data0=SCMASK[:],
                                      data1=BuR[:], initial=ZERO32[:],
                                      op0=ALU.mult, op1=ALU.add)
            vector.tensor_tensor_scan(out=CSI[:], data0=SCMASK[:],
                                      data1=QI[:], initial=ZERO32[:],
                                      op0=ALU.mult, op1=ALU.add)
            bcw(vector, c_t3, Ef[:, T - 1:T])
            vector.reciprocal(out=c_t3, in_=c_t3)
            bcw(vector, c_t4, Cf[:, T - 1:T])
            vector.tensor_tensor(out=c_A127r, in0=c_t3, in1=c_t4,
                                 op=ALU.mult)
            bcw(vector, c_t4, Sf[:, T - 1:T])
            vector.tensor_tensor(out=c_A127i, in0=c_t3, in1=c_t4,
                                 op=ALU.mult)
            bcw(vector, c_ARw, COLS4[:, 2:3])
            bcw(vector, c_AIw, COLS4[:, 3:4])
            cmulw(vector, c_A128r, c_A128i, c_A127r, c_A127i, c_ARw, c_AIw)
            cmulw(vector, c_u1, c_u2, c_A128r, c_A128i, c_A128r, c_A128i)
            cmulw(vector, c_A10r, c_A10i, c_u1, c_u2, c_u1, c_u2)
            cmulw(vector, c_u1, c_u2, c_A10r, c_A10i, c_A10r, c_A10i)
            vector.tensor_copy(out=c_A10r, in_=c_u1)
            vector.tensor_copy(out=c_A10i, in_=c_u2)
            vector.memset(c_cwR, 0.0)
            vector.memset(c_cwI, 0.0)
            for j in range(NCH):
                sl = slice(j * T, (j + 1) * T)
                p = j * T + T - 1
                vector.tensor_copy(out=C0R[:, sl], in_=c_cwR)
                vector.tensor_copy(out=C0I[:, sl], in_=c_cwI)
                bcw(vector, c_e1, CSR[:, p:p + 1])
                bcw(vector, c_e2, CSI[:, p:p + 1])
                cmulw(vector, c_acR, c_acI, c_A127r, c_A127i, c_e1, c_e2)
                cmulw(vector, c_u1, c_u2, c_A128r, c_A128i, c_cwR, c_cwI)
                vector.tensor_tensor(out=c_cwR, in0=c_acR, in1=c_u1,
                                     op=ALU.add)
                vector.tensor_tensor(out=c_cwI, in0=c_acI, in1=c_u2,
                                     op=ALU.add)
            vector.drain()
            vector.sem_inc(s_ve.h, 1)
            s_ve.n += 1

        blk.vector(vscan2)
        scan_target = s_ve.n

        def vpre(vector):
            vector.reciprocal(out=Ef[:], in_=Ef[:])
            for j in range(NCH):
                sl = slice(j * T, (j + 1) * T)
                cmulw(vector, c_acR, c_acI, c_ARw, c_AIw,
                      C0R[:, sl], C0I[:, sl])
                vector.tensor_tensor(out=ZR[:, sl], in0=CSR[:, sl],
                                     in1=c_acR, op=ALU.add)
                vector.tensor_tensor(out=ZI[:, sl], in0=CSI[:, sl],
                                     in1=c_acI, op=ALU.add)
            vector.tensor_tensor(out=Sf[:], in0=Sf[:], in1=Ef[:],
                                 op=ALU.mult)
            vector.tensor_tensor(out=Cf[:], in0=Cf[:], in1=Ef[:],
                                 op=ALU.mult)
            vector.tensor_tensor(out=XSR[:], in0=Cf[:], in1=ZR[:],
                                 op=ALU.mult)
            vector.tensor_tensor(out=Ef[:], in0=Sf[:], in1=ZI[:],
                                 op=ALU.mult)
            vector.tensor_tensor(out=XSR[:], in0=XSR[:], in1=Ef[:],
                                 op=ALU.subtract)
            vector.tensor_tensor(out=XSI[:], in0=Cf[:], in1=ZI[:],
                                 op=ALU.mult)
            vector.tensor_tensor(out=Ef[:], in0=Sf[:], in1=ZR[:],
                                 op=ALU.mult)
            vector.tensor_tensor(out=XSI[:], in0=XSI[:], in1=Ef[:],
                                 op=ALU.add)
            vector.tensor_copy(out=c_A127r, in_=c_ARw)
            vector.tensor_copy(out=c_A127i, in_=c_AIw)
            for j in range(NCH):
                sl = slice(j * T, (j + 1) * T)
                wr, wi = ((c_A127r, c_A127i) if j % 2 == 0
                          else (c_acR, c_acI))
                nr, ni = ((c_acR, c_acI) if j % 2 == 0
                          else (c_A127r, c_A127i))
                cmulw(vector, ZR[:, sl], ZI[:, sl], Cf[:, sl], Sf[:, sl],
                      wr, wi)
                if j + 1 < NCH:
                    cmulw(vector, nr, ni, wr, wi, c_A128r, c_A128i)
            for hb in range(NHB):
                for g in range(NG):
                    sl2 = slice(hb * TOK + g * GRP,
                                hb * TOK + g * GRP + GRP)
                    vector.scalar_tensor_tensor(
                        out=R[:, sl2], in0=XT[:, sl2],
                        scalar=DB[:, hb:hb + 1], in1=R[:, sl2],
                        op0=ALU.mult, op1=ALU.add)

        blk.vector(vpre)

        def gcoll(gpsimd):
            gpsimd.wait_ge(s_ve.h, scan_target)
            s_g.inc(gpsimd.dma_start(
                out=cc_in[0:1, 0:32].rearrange("p (a b) -> (p a) b", b=1),
                in_=c_cwR[:, 0:1]), 16)
            s_g.inc(gpsimd.dma_start(
                out=cc_in[0:1, 32:64].rearrange("p (a b) -> (p a) b", b=1),
                in_=c_cwI[:, 0:1]), 16)
            gpsimd.wait_ge(s_g.h, s_g.n)
            s_g.inc(gpsimd.collective_compute(
                "AllGather", ALU.bypass,
                replica_groups=[list(range(NCORES))],
                ins=[cc_in.opt()], outs=[cc_out.opt()]), 1)
            gpsimd.wait_ge(s_g.h, s_g.n)
            s_g.inc(gpsimd.dma_start(
                out=EG32[:],
                in_=cc_out.rearrange("r (c n) -> n (r c)", c=2),
                allow_slow_non_contiguous=True), 16)

        blk.gpsimd(gcoll)
        eg_target = s_g.n

        def vchain(vector):
            vector.wait_ge(s_g.h, eg_target)
            vector.memset(c_CwR, 0.0)
            vector.memset(c_CwI, 0.0)
            for b in range(NCORES):
                bcw(vector, c_e1, EG32[:, 2 * b:2 * b + 1])
                bcw(vector, c_e2, EG32[:, 2 * b + 1:2 * b + 2])
                if b == 0:
                    vector.tensor_copy(out=c_SR, in_=c_e1)
                    vector.tensor_copy(out=c_SI, in_=c_e2)
                else:
                    cmulw(vector, c_u1, c_u2, c_A10r, c_A10i, c_SR, c_SI)
                    vector.tensor_tensor(out=c_SR, in0=c_e1, in1=c_u1,
                                         op=ALU.add)
                    vector.tensor_tensor(out=c_SI, in0=c_e2, in1=c_u2,
                                         op=ALU.add)
                vector.scalar_tensor_tensor(out=c_CwR, in0=c_SR,
                                            scalar=SEL32[:, b:b + 1],
                                            in1=c_CwR, op0=ALU.mult,
                                            op1=ALU.add)
                vector.scalar_tensor_tensor(out=c_CwI, in0=c_SI,
                                            scalar=SEL32[:, b:b + 1],
                                            in1=c_CwI, op0=ALU.mult,
                                            op1=ALU.add)
            vector.tensor_scalar(out=c_u1, in0=c_CwI, scalar1=-1.0,
                                 scalar2=None, op0=ALU.mult)
            vector.scalar_tensor_tensor(out=XSR[:], in0=ZR[:],
                                        scalar=c_CwR[:, 0:1], in1=XSR[:],
                                        op0=ALU.mult, op1=ALU.add)
            vector.scalar_tensor_tensor(out=XSR[:], in0=ZI[:],
                                        scalar=c_u1[:, 0:1], in1=XSR[:],
                                        op0=ALU.mult, op1=ALU.add)
            vector.scalar_tensor_tensor(out=XSI[:], in0=ZR[:],
                                        scalar=c_CwI[:, 0:1], in1=XSI[:],
                                        op0=ALU.mult, op1=ALU.add)
            vector.scalar_tensor_tensor(out=XSI[:], in0=ZI[:],
                                        scalar=c_CwR[:, 0:1], in1=XSI[:],
                                        op0=ALU.mult, op1=ALU.add)
            vector.tensor_copy(out=XSB[0:32, :], in_=XSR[:])
            vector.tensor_copy(out=XSB[32:64, :], in_=XSI[:])
            vector.drain()
            vector.sem_inc(s_ve.h, 1)
            s_ve.n += 1

        blk.vector(vchain)
        xs_target = s_ve.n

        ssm_yd_base = s_yd.n
        ssm_mm_at = []

        def ty(tensor):
            tensor.wait_ge(s_ve.h, xs_target)
            yunit = 0
            for hb in range(NHB):
                for g in range(NG):
                    par = yunit % 2
                    if yunit >= 2:
                        tensor.wait_ge(s_yd.h, ssm_yd_base + yunit - 1)
                    ins = tensor.matmul(
                        out=P[4 + par][:, :],
                        lhsT=C2[:, hb * 128:(hb + 1) * 128],
                        rhs=XSB[:, g * GRP:(g + 1) * GRP],
                        start=True, stop=True)
                    s_mm.inc(ins)
                    ssm_mm_at.append(s_mm.n)
                    yunit += 1

        blk.tensor(ty)

        def vy(vector):
            yunit = 0
            for hb in range(NHB):
                for g in range(NG):
                    par = yunit % 2
                    vector.wait_ge(s_mm.h, ssm_mm_at[yunit])
                    sl = slice(hb * TOK + g * GRP, hb * TOK + g * GRP + GRP)
                    ins = vector.tensor_tensor(out=R[:, sl], in0=R[:, sl],
                                               in1=P[4 + par][:, :],
                                               op=ALU.add)
                    s_yd.inc(ins)
                    yunit += 1
                if hb % 4 == 3:
                    vector.drain()
                    vector.sem_inc(s_hdr.h, 1)
                    s_hdr.n += 1

        blk.vector(vy)
        return xs_target

    def tap(name):
        if name not in d_taps:
            return

        def stap(sync, name=name):
            s_st.inc(sync.dma_start(
                out=d_taps[name].rearrange("(b p) t -> p b t", p=128),
                in_=R[:].rearrange("p (b t) -> p b t", b=NHB)), 16)

        blk.sync(stap)

    xt1 = emit_ln(wait_rq=lambda e, q: e.wait_ge(s_rq[q].h, 16),
                  prefetch=lambda: (emit_gu_stage(0), emit_gu_stage(1)))
    emit_ffn(xt1, n_pre=2)
    if d_taps:
        barrier()
        tap("x1")
    hb2 = s_hdr.n - 4
    xt2 = emit_ln(wait_rq=lambda e, q, b=hb2: e.wait_ge(s_hdr.h, b + q + 1))
    ssm_free = emit_ssm(xt2)
    if d_taps:
        barrier()
        tap("x2")
    hb3 = s_hdr.n - 4
    xt3 = emit_ln(wait_rq=lambda e, q, b=hb3: e.wait_ge(s_hdr.h, b + q + 1),
                  prefetch=lambda: (emit_gu_stage(32, (s_ve, ssm_free)),
                                    emit_gu_stage(33)))
    emit_ffn(xt3, n_pre=2)
    if d_taps:
        barrier()

    def store(sync):
        base = s_hdr.n - 4
        for q in range(4):
            sync.wait_ge(s_hdr.h, base + q + 1)
            dst = d_yT[q * 512:(q + 1) * 512, :].rearrange(
                "(b p) t -> p b t", p=128)
            src = R[:, q * 4 * TOK:(q + 1) * 4 * TOK].rearrange(
                "p (b t) -> p b t", b=4)
            s_st.inc(sync.dma_start(out=dst, in_=src), 16)
        sync.wait_ge(s_st.h, s_st.n)

    blk.sync(store)
    blk.__exit__(None, None, None)
    return nc


def _host_inputs(inputs, taps=()):
    asc = np.ascontiguousarray
    u = inputs["L0_attn_u"]
    def lowbyte(w):
        w = np.ascontiguousarray(np.asarray(w))
        return w.view(np.uint8).reshape(w.shape[0], w.shape[1], 4)[:, :, 0]

    wgT = asc(lowbyte(inputs["L0_ffn_gate_w"]).T)
    wuT = asc(lowbyte(inputs["L0_ffn_up_w"]).T)
    wdT = asc(lowbyte(inputs["L0_ffn_down_w"]).T)
    gsB = asc(np.asarray(inputs["L0_ffn_gate_scale"]).reshape(NIB, 128).T)
    usB = asc(np.asarray(inputs["L0_ffn_up_scale"]).reshape(NIB, 128).T)
    dsB = asc(np.asarray(inputs["L0_ffn_down_scale"]).reshape(NHB, 128).T)
    DBh = asc(np.asarray(inputs["L0_attn_D"]).reshape(NHB, 128).T)
    BreT = asc(np.asarray(inputs["L0_attn_B_re"]).T)
    BimT = asc(np.asarray(inputs["L0_attn_B_im"]).T)
    CreT = asc(np.asarray(inputs["L0_attn_C_re"]).T)
    CimT = asc(np.asarray(inputs["L0_attn_C_im"]).T)

    def padrow(v, n=RW_W):
        out = np.zeros((1, n), np.float32)
        v = np.asarray(v).reshape(-1)
        out[0, :v.size] = v
        return out

    Lu = padrow(inputs["L0_attn_Lambda_unconstrained_re"])
    Lim = padrow(inputs["L0_attn_Lambda_im"])
    logD = padrow(np.asarray(inputs["L0_attn_log_Delta"]).reshape(-1)[:32])
    iotaB = asc(np.broadcast_to((np.arange(TOK) % T).astype(BFNP),
                                (32, TOK)))
    scMask = np.ones((32, TOK), np.float32)
    scMask[:, 0::T] = 0.0
    onesW = np.ones((64, T), np.float32)
    onesB = np.ones((128, 1), BFNP)
    onesR = np.ones((1, 128), np.float32)
    in_maps = []
    for c in range(NCORES):
        xT = asc(np.asarray(u[0, c * TOK:(c + 1) * TOK, :]).T)
        sel32 = np.zeros((32, 8), np.float32)
        if c > 0:
            sel32[:, c - 1] = 1.0
        in_maps.append(dict(
            xT=xT, wgT=wgT, wuT=wuT, wdT=wdT, gsB=gsB, usB=usB, dsB=dsB,
            DB=DBh, BreT=BreT, BimT=BimT, CreT=CreT, CimT=CimT, Lu=Lu,
            Lim=Lim, logD=logD, sel32=sel32, iotaB=iotaB, scMask=scMask,
            onesW=onesW, onesB=onesB, onesR=onesR))
    return in_maps


def kernel(**inputs):
    nc = build_program(taps=())
    in_maps = _host_inputs(inputs)
    res = run_bass_kernel_spmd(nc, in_maps, core_ids=list(range(NCORES)))
    parts = [np.asarray(res.results[c]["yT"]).T for c in range(NCORES)]
    return np.concatenate(parts, axis=0).reshape(1, 8192, 2048)
